# version 23
# baseline (speedup 1.0000x reference)
import math
import numpy as np
import ml_dtypes
import concourse.bass as bass
import concourse.mybir as mybir
from concourse.bass_utils import run_bass_kernel_spmd

F32 = mybir.dt.float32
BF16 = mybir.dt.bfloat16
AF = mybir.ActivationFunctionType
ALU = mybir.AluOpType
AX = mybir.AxisListType

P = 128
S = 4096
D = 2048
H = 8
DH = 128
NFC = D // P
NI = 5632 // P
NTQ = S // 256
SCALE = DH ** -0.5
LAMBDA_INIT = 0.8 - 0.6 * math.exp(-0.3 * 12)
EPS = 1e-5
NPT = 4

ENGINES = ("sync", "gpsimd", "tensor", "vector", "scalar")

SEM_NAMES = [
    "bar",
    "xt_ld", "w_ld0", "w_ld1", "wv_ld0", "wv_ld1", "wv_ld2",
    "grp", "sq_done", "statmm", "red",
    "stg0", "stg1", "stg2", "stg3", "stgr",
    "vmax", "mhat", "lamf", "bc", "prep",
    "hl0", "hl1",
    "scg", "pr", "pm",
    "psf0", "psf1", "psf2", "psf3",
    "pvall", "obfree", "oeread", "ssum", "rstd", "an", "tp", "atcp", "ast",
    "dwc_ld0", "dwc_ld1", "dwc_st0", "dwc_st1",
    "wo_ld", "wo_sc", "at_ld0", "at_ld1", "xt_r0", "xt_r1",
    "mmf", "evf", "stat2", "rows", "bc2", "app",
    "h1g0", "h1g1",
    "h1pl", "h1pl2", "gw0", "gw1", "gw2s", "gw3s", "uw0", "uw1", "uw2s", "uw3s", "gu_grp", "sg_done",
    "uu_done", "hdn_done",
    "hdng0", "hdng1", "hdng2", "hdng3",
    "hdn_ld0", "hdn_ld1", "dwt0", "dwt1", "dwt2s", "h1r0", "h1r1",
    "mm4", "ev4", "stat4", "rows4", "bc4", "app4",
    "y0", "y1", "y2", "y3",
]

SEM_ALIAS = {}
for _b in ("scg",):
    SEM_ALIAS[_b + "0"] = _b
    SEM_ALIAS[_b + "1"] = _b
SEM_ALIAS["gw2"] = "gw2s"
SEM_ALIAS["uw2"] = "uw2s"
SEM_ALIAS["gw3"] = "gw3s"
SEM_ALIAS["uw3"] = "uw3s"
SEM_ALIAS["dwt2"] = "dwt2s"
for _b in ("pr", "pm", "stgr"):
    for _i in range(4):
        SEM_ALIAS[_b + str(_i)] = _b

_DT_SIZE = {F32: 4, BF16: 2}


def _ceil32(x):
    return (x + 31) & ~31


class Arena:
    def __init__(self, nc, base, top):
        self.nc, self.base, self.top, self.cur = nc, base, top, base
        self.uid = 0

    def reset(self):
        self.cur = self.base

    def alloc(self, name, shape, dtype):
        nb = _DT_SIZE[dtype]
        for s_ in shape[1:]:
            nb *= s_
        t = self.nc.alloc_sbuf_tensor_at(f"{name}_{self.uid}", list(shape), dtype,
                                         offset=self.cur)
        self.uid += 1
        self.cur += _ceil32(nb)
        assert self.cur <= self.top, f"arena overflow at {name}: {self.cur} > {self.top}"
        return t


class Em:
    def __init__(self, nc, name, eng, sems, counts):
        self.nc, self.name, self.eng, self.sems, self.c = nc, name, eng, sems, counts
        self.dma_used = {}

    def i(self, sem, inst=None, k=1):
        sem = SEM_ALIAS.get(sem, sem)
        self.c[sem] += k
        if inst is not None:
            inst.then_inc(self.sems[sem], k)

    def di(self, ename, sem, k=1):
        sem = SEM_ALIAS.get(sem, sem)
        self.c[sem] += k
        if self.name == ename:
            self.eng.sem_inc(self.sems[sem], k)

    def wt(self, ename, sem, target):
        sem = SEM_ALIAS.get(sem, sem)
        if self.name == ename and target > 0:
            self.eng.wait_ge(self.sems[sem], target)

    def w(self, ename, sem, slack=0):
        sem = SEM_ALIAS.get(sem, sem)
        if self.name == ename:
            tgt = self.c[sem] - slack
            if tgt > 0:
                self.eng.wait_ge(self.sems[sem], tgt)

    def dma(self, qname, sem, out, in_, **kw):
        inst = None
        if self.name == qname:
            inst = self.eng.dma_start(out=out, in_=in_, **kw)
        self.dma_used[SEM_ALIAS.get(sem, sem)] = qname
        self.i(sem, inst, 16)

    def mm(self, out, lhsT, rhs, start, stop, sem=None):
        inst = None
        if self.name == "tensor":
            inst = self.eng.matmul(out, lhsT, rhs, start=start, stop=stop)
        if sem is not None:
            self.i(sem, inst)
        return inst




def build_nc(debug=False):
    nc = bass.Bass("TRN2", target_bir_lowering=False, debug=False,
                   detect_race_conditions=False)
    scratch_kind = "ExternalOutput" if debug else "Internal"

    xT = nc.dram_tensor("xT", [D, S], F32, kind="ExternalInput").ap()
    wq = nc.dram_tensor("wq", [16, P, NFC, P], F32, kind="ExternalInput").ap()
    wk = nc.dram_tensor("wk", [16, P, NFC, P], F32, kind="ExternalInput").ap()
    wv = nc.dram_tensor("wv", [4, P, NFC, 512], F32, kind="ExternalInput").ap()
    wo = nc.dram_tensor("wo", [D, D], F32, kind="ExternalInput").ap()
    gwT = nc.dram_tensor("gwT", [NI, P, NFC, P], F32, kind="ExternalInput").ap()
    uwT = nc.dram_tensor("uwT", [NI, P, NFC, P], F32, kind="ExternalInput").ap()
    dwT = nc.dram_tensor("dwT", [NFC, P, NI, P], F32, kind="ExternalInput").ap()
    lam = nc.dram_tensor("lam", [1, 512], F32, kind="ExternalInput").ap()
    subw2 = nc.dram_tensor("subw2", [P, 2], F32, kind="ExternalInput").ap()
    gs_t = nc.dram_tensor("gs_t", [P, NI], F32, kind="ExternalInput").ap()
    us_t = nc.dram_tensor("us_t", [P, NI], F32, kind="ExternalInput").ap()
    ds_t = nc.dram_tensor("ds_t", [P, NFC], F32, kind="ExternalInput").ap()
    yT = nc.dram_tensor("yT", [D, S], F32, kind="ExternalOutput").ap()

    q_scr = nc.dram_tensor("q_scr", [16, P, S], BF16, kind=scratch_kind).ap()
    k_scr = nc.dram_tensor("k_scr", [16, P, S], BF16, kind=scratch_kind).ap()
    v_scr = nc.dram_tensor("v_scr", [4, 32, P, 512], BF16, kind=scratch_kind).ap()
    a_scr = nc.dram_tensor("a_scr", [8, P, 16, 512], BF16, kind=scratch_kind).ap()
    h1f_scr = nc.dram_tensor("h1f_scr", [NFC, P, S], F32, kind=scratch_kind).ap()
    h1b_scr = nc.dram_tensor("h1b_scr", [NFC, P, S], BF16, kind="Internal").ap()
    hdn_scr = nc.dram_tensor("hdn_scr", [8, P, NI, 512], BF16, kind=scratch_kind).ap()
    dw_bf = nc.dram_tensor("dw_bf", [NFC, P, NI, P], BF16, kind="Internal").ap()
    gw_bf = nc.dram_tensor("gw_bf", [NI, P, NFC * P], BF16, kind="Internal").ap()
    uw_bf = nc.dram_tensor("uw_bf", [NI, P, NFC * P], BF16, kind="Internal").ap()
    dbg = nc.dram_tensor("dbg", [P, 64], F32, kind=scratch_kind).ap() if debug else None
    dbg2 = nc.dram_tensor("dbg2", [P, 2, 512], F32, kind=scratch_kind).ap() if debug else None
    dbg3 = nc.dram_tensor("dbg3", [P, 32, 257], F32, kind=scratch_kind).ap() if debug else None
    dbg4 = nc.dram_tensor("dbg4", [P, 2, 512], F32, kind=scratch_kind).ap() if debug else None
    dbg5 = nc.dram_tensor("dbg5", [P, 4, 257], F32, kind=scratch_kind).ap() if debug else None
    dbg6 = nc.dram_tensor("dbg6", [P, 2, 256], F32, kind=scratch_kind).ap() if debug else None

    ident_h = nc.inline_tensor(np.eye(P, dtype=ml_dtypes.bfloat16), name="identc_i")
    onesc_h = nc.inline_tensor(np.ones((P, 1), dtype=ml_dtypes.bfloat16), name="onescol_i")
    onesr_h = nc.inline_tensor(np.ones((1, P), dtype=np.float32), name="onesrow_i")
    negr_h = nc.inline_tensor(np.full((1, P), -1.0, dtype=np.float32), name="negrow_i")
    m_np = np.zeros((P, 2, 512), dtype=ml_dtypes.bfloat16)
    for d_ in range(2):
        pat = ((np.arange(P)[:, None] + 128 * d_) <= np.arange(256)[None, :])
        m_np[:, d_, 0:256] = pat.astype(ml_dtypes.bfloat16)
        m_np[:, d_, 256:512] = pat.astype(ml_dtypes.bfloat16)
    mask_h = nc.inline_tensor(m_np, name="maskc_i")

    ident = nc.alloc_sbuf_tensor("ident", [P, P], BF16)
    onescol = nc.alloc_sbuf_tensor("onescol", [P, 1], BF16)
    onesrow = nc.alloc_sbuf_tensor("onesrow", [1, P], F32)
    negrow = nc.alloc_sbuf_tensor("negrow", [1, P], F32)
    masks = nc.alloc_sbuf_tensor("masks", [P, 2, 512], BF16)
    lam_sb = nc.alloc_sbuf_tensor("lam_sb", [1, 512], F32)
    w2_sb = nc.alloc_sbuf_tensor("w2_sb", [P, 2], F32)
    gs_sb = nc.alloc_sbuf_tensor("gs_sb", [P, NI], F32)
    us_sb = nc.alloc_sbuf_tensor("us_sb", [P, NI], F32)
    ds_sb = nc.alloc_sbuf_tensor("ds_sb", [P, NFC], F32)
    zeroc = nc.alloc_sbuf_tensor("zeroc", [P, 1], F32)
    epsc = nc.alloc_sbuf_tensor("epsc", [P, 1], F32)
    zero1 = nc.alloc_sbuf_tensor("zero1", [1, 1], F32)
    eps1 = nc.alloc_sbuf_tensor("eps1", [1, 1], F32)
    c0r = nc.alloc_sbuf_tensor("c0r", [1, 1], F32)
    m60 = nc.alloc_sbuf_tensor("m60", [1, 1], F32)
    nmt = nc.alloc_sbuf_tensor("nmt", [1, 256], F32)
    pmax = nc.alloc_sbuf_tensor("pmax", [1, 32], F32)
    prod = nc.alloc_sbuf_tensor("prod", [1, 16], F32)
    lamp = nc.alloc_sbuf_tensor("lamp", [1, 256], F32)
    w2b = nc.alloc_sbuf_tensor("w2b", [P, 2], F32)
    tmp11 = nc.alloc_sbuf_tensor("tmp11", [1, 8], F32)
    expbias = nc.alloc_sbuf_tensor("expbias", [P, 1], F32)
    lam_bc = nc.alloc_sbuf_tensor("lam_bc", [P, 1], F32)
    dumv = nc.alloc_sbuf_tensor("dumv", [1, 1], F32)
    dums = nc.alloc_sbuf_tensor("dums", [1, 1], F32)
    m_row = nc.alloc_sbuf_tensor("m_row", [1, 512], F32)
    vv_row = nc.alloc_sbuf_tensor("vv_row", [1, 512], F32)
    r_row = nc.alloc_sbuf_tensor("r_row", [1, 512], F32)
    b_row = nc.alloc_sbuf_tensor("b_row", [1, 512], F32)

    ar = Arena(nc, _ceil32(nc.sbuf_base), nc.sbuf_top)

    ar.reset()
    xt_bf = ar.alloc("xt_bf", [P, NFC, S], BF16)
    wqk_t = [ar.alloc("wqk_t", [P, NFC, P], BF16) for _ in range(2)]
    wv_t = [ar.alloc("wv_t", [P, NFC, 512], BF16) for _ in range(2)]
    stg = [ar.alloc("stg", [P, 512], BF16) for _ in range(4)]
    sqb = [ar.alloc("sqb", [P, 512], BF16) for _ in range(2)]

    ar.reset()
    qk_buf = [[ar.alloc("qk", [P, S], BF16) for _ in range(4)] for _ in range(2)]
    v_buf = [ar.alloc("vbuf", [P, 32, 257], BF16) for _ in range(2)]
    p_t = [ar.alloc("pt", [P, 512], BF16) for _ in range(NPT)]
    pmask_t = [ar.alloc("pmt", [P, 512], BF16) for _ in range(2)]
    o_c = [[ar.alloc("oc", [P, 257], F32) for _ in range(2)] for _ in range(2)]
    t2_sb = [ar.alloc("t2s", [P, 256], F32) for _ in range(2)]
    a_sb = [ar.alloc("a_sb", [P, 256], F32) for _ in range(2)]
    sqe = [ar.alloc("sqe", [P, 256], F32) for _ in range(2)]
    r1s = [ar.alloc("r1s", [P, 1], F32) for _ in range(2)]
    r2s = [ar.alloc("r2s", [P, 1], F32) for _ in range(2)]
    r2l = [ar.alloc("r2l", [P, 1], F32) for _ in range(2)]
    ssm = [ar.alloc("ssm", [P, 1], F32) for _ in range(2)]
    rst = [ar.alloc("rst", [P, 1], F32) for _ in range(2)]
    an_sb = [[ar.alloc("an", [P, 256], BF16) for _ in range(2)] for _ in range(3)]
    at_stage = [ar.alloc("at_st", [P, 512], BF16) for _ in range(2)]
    dwc_stage = [ar.alloc("dwc", [P, NI * P], BF16) for _ in range(2)]

    ar.reset()
    wo_sb = ar.alloc("wo_sb", [P, NFC, D], BF16)
    at_t = [ar.alloc("at_t", [P, NFC, 512], BF16) for _ in range(2)]
    xt_r = [ar.alloc("xt_r", [P, 512], F32) for _ in range(2)]
    xres = [ar.alloc("xres", [P, NFC, 512], F32) for _ in range(2)]
    xrb = [ar.alloc("xrb", [P, 512], BF16) for _ in range(2)]
    sq3 = [ar.alloc("sq3", [P, 512], BF16) for _ in range(2)]
    ln_tmp = [ar.alloc("ln_tmp", [P, 512], F32) for _ in range(2)]
    wo_stg = [ar.alloc("wo_stg", [P, D], BF16) for _ in range(2)]
    h1f_st = [ar.alloc("h1f_st", [P, 512], F32) for _ in range(2)]
    h1b_st = [ar.alloc("h1b_st", [P, 512], BF16) for _ in range(2)]

    ar.reset()
    h1_pl = ar.alloc("h1_pl", [P, NFC, S], BF16)
    gw_t = [ar.alloc("gw_t", [P, NFC, P], BF16) for _ in range(4)]
    uw_t = [ar.alloc("uw_t", [P, NFC, P], BF16) for _ in range(4)]
    sg_sb = [ar.alloc("sg", [P, 512], F32) for _ in range(2)]
    uu_sb = [ar.alloc("uu", [P, 512], F32) for _ in range(2)]
    hdn_st = [ar.alloc("hdn_st", [P, 512], BF16) for _ in range(4)]

    ar.reset()
    hdn_t = [ar.alloc("hdn_t", [P, NI, 512], BF16) for _ in range(2)]
    dw_t = [ar.alloc("dw_t", [P, NI, P], BF16) for _ in range(2)]
    h1r = [ar.alloc("h1r", [P, 512], F32) for _ in range(2)]
    xres2 = [ar.alloc("xres2", [P, NFC, 512], F32) for _ in range(2)]
    xrb4 = [ar.alloc("xrb4", [P, 512], BF16) for _ in range(2)]
    sq4 = [ar.alloc("sq4", [P, 512], BF16) for _ in range(2)]
    ln_tmp4 = [ar.alloc("ln_tmp4", [P, 512], F32) for _ in range(2)]
    y_st = [ar.alloc("y_st", [P, 512], F32) for _ in range(2)]

    ps_f = [nc.alloc_psum_tensor(f"psf{i}", [P, 512], F32) for i in range(7)]
    ps_t = nc.alloc_psum_tensor("pst", [P, 512], BF16)

    import contextlib
    es = contextlib.ExitStack()
    sems = {n: es.enter_context(nc.semaphore(n)) for n in SEM_NAMES}
    C0 = math.log(1.02 * SCALE)

    def emit_all(E: Em):
        t = E.eng if E.name == "tensor" else None
        v = E.eng if E.name == "vector" else None
        s = E.eng if E.name == "scalar" else None

        def phase_barrier():
            for sem in sorted(E.dma_used):
                E.w(E.name, sem)
            inst = None
            if t:
                inst = t.matmul(ps_f[6][0:1, 0:1], onesrow[0:1, 0:1],
                                onesrow[0:1, 0:1], start=True, stop=True)
            E.i("bar", inst)
            inst = None
            if v:
                inst = v.tensor_copy(out=dumv[:], in_=onesrow[0:1, 0:1])
            E.i("bar", inst)
            inst = None
            if s:
                inst = s.activation(out=dums[:], in_=onesrow[0:1, 0:1], func=AF.Copy,
                                    bias=0.0, scale=1.0)
            E.i("bar", inst)
            E.eng.wait_ge(E.sems["bar"], E.c["bar"])

        E.dma("sync", "stg0", ident[:], ident_h.ap())
        E.dma("sync", "stg0", onescol[:], onesc_h.ap())
        E.dma("sync", "stg0", onesrow[:], onesr_h.ap())
        E.dma("sync", "stg0", negrow[:], negr_h.ap())
        E.dma("sync", "stg0", masks[:], mask_h.ap())
        E.dma("sync", "stg0", lam_sb[:], lam)
        E.dma("sync", "stg0", w2_sb[:], subw2)
        E.dma("sync", "stg0", gs_sb[:], gs_t)
        E.dma("sync", "stg0", us_sb[:], us_t)
        E.dma("sync", "stg0", ds_sb[:], ds_t)
        if v:
            v.memset(zeroc[:], 0.0)
            v.memset(epsc[:], EPS)
            v.memset(zero1[:], 0.0)
            v.memset(eps1[:], EPS)
            v.memset(c0r[:], C0)
            v.memset(m60[:], -60.0)
            inst = v.memset(nmt[:], 0.0)
        else:
            inst = None
        E.i("red", inst)
        E.w("vector", "stg0")
        if v:
            inst = v.tensor_scalar_mul(w2b[:], w2_sb[:], 1.0 - LAMBDA_INIT)
        else:
            inst = None
        E.i("red", inst)
        E.w("scalar", "red")
        E.w("tensor", "stg0")
        E.dma("gpsimd", "w_ld0", wqk_t[0][:], wq[0])
        E.dma("gpsimd", "w_ld1", wqk_t[1][:], wq[1])
        xr_ap = xT.rearrange("(c p) n -> p c n", p=P)
        xt_pool = ["xt_ld", "hl0", "hl1", "at_ld0",
                   "at_ld1", "hdn_ld0", "hdn_ld1", "dwt0"]
        xt_tg = []
        for tc_ in range(8):
            sm = xt_pool[tc_]
            E.dma("gpsimd", sm, xt_bf[:, :, tc_ * 512:(tc_ + 1) * 512],
                  xr_ap[:, :, tc_ * 512:(tc_ + 1) * 512])
            xt_tg.append((sm, E.c[SEM_ALIAS.get(sm, sm)]))

        grp = 0
        for pi, (w_ap, scr) in enumerate([(wq, q_scr), (wk, k_scr)]):
            for oc in range(16):
                li = pi * 16 + oc
                slot = li % 2
                wsem = f"w_ld{slot}"
                if li >= 2:
                    E.w("gpsimd", "grp", slack=8)
                    E.dma("gpsimd", wsem, wqk_t[slot][:], w_ap[oc])
                E.w("tensor", wsem)
                for tc in range(8):
                    if li == 0:
                        E.wt("tensor", xt_tg[tc][0], xt_tg[tc][1])
                    bank = grp % 2
                    E.w("tensor", "sq_done", slack=1)
                    last = None
                    for c in range(NFC):
                        last = E.mm(ps_f[bank][:], wqk_t[slot][:, c, :],
                                    xt_bf[:, c, tc * 512:(tc + 1) * 512],
                                    start=(c == 0), stop=(c == NFC - 1))
                    E.i("grp", last)
                    if grp > 0:
                        E.w("tensor", "sq_done")
                        E.w("tensor", "red")
                        E.mm(ps_f[2][0:1, :], onescol[:], sqb[(grp - 1) % 2][:],
                             start=True, stop=True, sem="statmm")
                    E.w("scalar", "grp")
                    stslot = grp % 4
                    E.w("scalar", f"stg{stslot}")
                    cinst = None
                    if s:
                        cinst = s.activation(out=stg[stslot][:], in_=ps_f[bank][:],
                                             func=AF.Copy, bias=0.0, scale=1.0)
                    E.i(f"stgr{stslot}", cinst)
                    E.w("scalar", "statmm")
                    inst = None
                    if s:
                        inst = s.activation(out=sqb[grp % 2][:], in_=ps_f[bank][:],
                                            func=AF.Square, bias=zeroc[:], scale=1.0)
                    E.i("sq_done", inst)
                    if grp > 0:
                        E.w("vector", "statmm")
                        inst = None
                        if v:
                            inst = v.reduce_max(nmt[0:1, grp - 1:grp], ps_f[2][0:1, :],
                                                axis=AX.X)
                        E.i("red", inst)
                    E.w("sync", f"stgr{stslot}")
                    E.dma("sync", f"stg{stslot}", scr[oc][:, tc * 512:(tc + 1) * 512],
                          stg[stslot][:])
                    grp += 1
        E.w("tensor", "sq_done")
        E.w("tensor", "red")
        E.mm(ps_f[2][0:1, :], onescol[:], sqb[(grp - 1) % 2][:], start=True, stop=True,
             sem="statmm")
        E.w("vector", "statmm")
        inst = None
        if v:
            inst = v.reduce_max(nmt[0:1, grp - 1:grp], ps_f[2][0:1, :], axis=AX.X)
        E.i("red", inst)

        if v:
            for pl in range(32):
                v.reduce_max(pmax[0:1, pl:pl + 1], nmt[0:1, pl * 8:(pl + 1) * 8],
                             axis=AX.X)
            for pl in range(16):
                v.tensor_mul(prod[0:1, pl:pl + 1], pmax[0:1, pl:pl + 1],
                             pmax[0:1, 16 + pl:17 + pl])
            v.tensor_mul(lamp[0:1, 0:P], lam_sb[0:1, 0:P], lam_sb[0:1, P:2 * P])
            v.tensor_mul(lamp[0:1, P:2 * P], lam_sb[0:1, 2 * P:3 * P],
                         lam_sb[0:1, 3 * P:4 * P])
            v.reduce_max(tmp11[0:1, 1:2], prod[0:1, 0:16], axis=AX.X)
            v.reduce_sum(tmp11[0:1, 2:3], lamp[0:1, 0:P], axis=AX.X)
            inst = v.reduce_sum(tmp11[0:1, 3:4], lamp[0:1, P:2 * P], axis=AX.X)
        else:
            inst = None
        E.i("vmax", inst)
        E.w("scalar", "vmax")
        inst = None
        if s:
            s.activation(out=tmp11[0:1, 1:2], in_=tmp11[0:1, 1:2], func=AF.Ln,
                         bias=zero1[:], scale=1.0)
            s.activation(out=tmp11[0:1, 1:2], in_=tmp11[0:1, 1:2], func=AF.Exp,
                         bias=c0r[:], scale=0.5)
            s.activation(out=tmp11[0:1, 1:2], in_=tmp11[0:1, 1:2], func=AF.Relu,
                         bias=m60[:], scale=1.0)
            s.activation(out=tmp11[0:1, 2:3], in_=tmp11[0:1, 2:3], func=AF.Exp,
                         bias=zero1[:], scale=1.0)
            inst = s.activation(out=tmp11[0:1, 3:4], in_=tmp11[0:1, 3:4], func=AF.Exp,
                                bias=zero1[:], scale=1.0)
        E.i("mhat", inst)
        E.w("vector", "mhat")
        inst = None
        if v:
            v.tensor_sub(tmp11[0:1, 6:7], tmp11[0:1, 2:3], tmp11[0:1, 3:4])
            v.tensor_scalar_mul(tmp11[0:1, 1:2], tmp11[0:1, 1:2], -1.0)
            inst = v.tensor_scalar_add(tmp11[0:1, 6:7], tmp11[0:1, 6:7], LAMBDA_INIT)
        E.i("lamf", inst)
        E.w("tensor", "lamf")
        E.mm(ps_f[6][:, 0:1], onesrow[:], tmp11[0:1, 1:2], start=True, stop=True)
        E.mm(ps_f[6][:, 1:2], onesrow[:], tmp11[0:1, 6:7], start=True, stop=True, sem="bc")
        E.w("vector", "bc")
        inst = None
        if v:
            v.tensor_copy(out=expbias[:], in_=ps_f[6][:, 0:1])
            inst = v.tensor_copy(out=lam_bc[:], in_=ps_f[6][:, 1:2])
        E.i("prep", inst)
        if debug:
            E.w("sync", "prep")
            E.dma("sync", "stg0", dbg[:, 0:1], expbias[:], allow_slow_non_contiguous=True)
            E.dma("sync", "stg0", dbg[:, 1:2], lam_bc[:], allow_slow_non_contiguous=True)
            E.dma("gpsimd", "xt_ld", dbg2[:], masks[:])

        E.w("tensor", "sq_done")
        vg = 0
        for hp in range(4):
            slot = hp % 2
            wsem = f"wv_ld{slot}"
            E.w("gpsimd", "grp", slack=32)
            E.dma("gpsimd", wsem, wv_t[slot][:], wv[hp])
            E.w("tensor", wsem)
            for tc in range(32):
                bank = vg % 2
                E.w("tensor", "stgr", slack=1)
                last = None
                for c in range(NFC):
                    last = E.mm(ps_f[bank][:], xt_bf[:, c, tc * P:(tc + 1) * P],
                                wv_t[slot][:, c, :], start=(c == 0), stop=(c == NFC - 1))
                E.i("grp", last)
                E.w("scalar", "grp")
                stslot = vg % 4
                E.w("scalar", f"stg{stslot}")
                inst = None
                if s:
                    inst = s.activation(out=stg[stslot][:], in_=ps_f[bank][:], func=AF.Copy,
                                        bias=0.0, scale=1.0)
                E.i(f"stgr{stslot}", inst)
                E.w("sync", f"stgr{stslot}")
                E.dma("sync", f"stg{stslot}", v_scr[hp, tc][:], stg[stslot][:])
                vg += 1

        phase_barrier()

        inst = None
        if v:
            v.memset(v_buf[0][:, :, 256:257], 1.0)
            inst = v.memset(v_buf[1][:, :, 256:257], 1.0)
        E.i("prep", inst)
        E.w("scalar", "prep")
        E.w("tensor", "prep")

        def head_load_items(h):
            slot = h % 2
            sem = f"hl{slot}"
            items = []
            for comp in range(2):
                items.append((sem, qk_buf[slot][comp][:], q_scr[2 * h + comp]))
                items.append((sem, qk_buf[slot][2 + comp][:], k_scr[2 * h + comp]))
            off = (h % 2) * 256
            for tcv in range(32):
                items.append((sem, v_buf[slot][:, tcv, 0:256],
                              v_scr[h // 2, tcv][:, off:off + 256]))
            return items

        def head_load(h):
            E.w("sync", "obfree")
            for sem, out, in_ in head_load_items(h):
                E.dma("sync", sem, out, in_)

        pend_tail = []
        pend_act = []

        def emit_act2(st):
            hh, jj, jg_, ssum_tgt = st
            E.wt("scalar", "ssum", ssum_tgt)
            inst = None
            if s:
                s.activation(out=rst[0][:], in_=ssm[0][:], func=AF.Ln, bias=epsc[:],
                             scale=1.0 / 256.0)
                s.activation(out=rst[1][:], in_=ssm[1][:], func=AF.Ln, bias=epsc[:],
                             scale=1.0 / 256.0)
                s.activation(out=rst[0][:], in_=rst[0][:], func=AF.Exp,
                             bias=zeroc[:], scale=-0.5)
                inst = s.activation(out=rst[1][:], in_=rst[1][:], func=AF.Exp,
                                    bias=zeroc[:], scale=-0.5)
            E.i("rstd", inst)
            E.w("vector", "rstd")
            E.w("vector", "tp")
            inst = None
            if v:
                v.tensor_scalar(out=an_sb[jg_ % 3][0][:], in0=a_sb[0][:],
                                scalar1=rst[0][:], scalar2=None, op0=ALU.mult)
                inst = v.tensor_scalar(out=an_sb[jg_ % 3][1][:], in0=a_sb[1][:],
                                       scalar1=rst[1][:], scalar2=None, op0=ALU.mult)
            E.i("an", inst)
            pend_tail.append((hh, jj, jg_, E.c["an"]))

        pc_jobs = ([("dw", ho_) for ho_ in range(NFC)]
                   + [("gw", i_) for i_ in range(NI)]
                   + [("uw", i_) for i_ in range(NI)])
        pc_sched = {h_: pc_jobs[(len(pc_jobs) * h_) // H:(len(pc_jobs) * (h_ + 1)) // H]
                    for h_ in range(H)}

        def emit_tail2(st):
            hh, jj, jg_, an_tgt = st
            E.wt("tensor", "an", an_tgt)
            E.w("tensor", "atcp")
            for sl in range(2):
                last = None
                if t:
                    t.transpose(ps_t[:, sl * P:(sl + 1) * P],
                                an_sb[jg_ % 3][sl][:, 0:P], ident[:])
                    last = t.transpose(ps_t[:, 256 + sl * P:256 + (sl + 1) * P],
                                       an_sb[jg_ % 3][sl][:, P:2 * P], ident[:])
                E.i("tp", last)
            E.w("vector", "tp")
            E.w("vector", "ast")
            inst = None
            if v:
                inst = v.tensor_copy(out=at_stage[jj % 2][:], in_=ps_t[:])
            E.i("atcp", inst)
            E.w("sync", "atcp")
            tj, oj = (jj * 256) // 512, (jj * 256) % 512
            E.dma("sync", "ast", a_scr[tj][:, 2 * hh, oj:oj + 256],
                  at_stage[jj % 2][:, 0:256])
            E.dma("sync", "ast", a_scr[tj][:, 2 * hh + 1, oj:oj + 256],
                  at_stage[jj % 2][:, 256:512])

        ktg = [0]
        head_load(0)
        if debug:
            E.w("gpsimd", "hl0")
            E.dma("gpsimd", "xt_ld", dbg3[:], v_buf[0][:])

        def emit_pv(st):
            c_, srct_, prt_, pmt_, nkt_, vc_ = st
            if pmt_ is not None:
                E.wt("tensor", "pm", pmt_)
            else:
                E.wt("tensor", "pr", prt_)
            if c_ == 0:
                E.w("tensor", "obfree")
            last = None
            for b_ in range(2):
                for sl in range(2):
                    last = E.mm(ps_f[2 + 2 * b_ + sl][:, 0:257],
                                srct_[:, b_ * 256 + sl * P:b_ * 256 + (sl + 1) * P],
                                vc_[:, c_, :], start=(c_ == 0),
                                stop=(c_ == nkt_ - 1))
            E.i(f"psf{c_ % NPT}", last)

        def emit_epilogue(st):
            hh, jj = st
            nkt_ = 2 * jj + 2
            lastslot = (nkt_ - 1) % NPT
            E.w("vector", f"psf{lastslot}")
            insts = [None, None, None, None]
            if v:
                insts[0] = v.tensor_copy(out=o_c[0][0][:], in_=ps_f[2][:, 0:257])
                insts[1] = v.tensor_copy(out=o_c[0][1][:], in_=ps_f[3][:, 0:257])
                insts[2] = v.tensor_copy(out=o_c[1][0][:], in_=ps_f[4][:, 0:257])
                insts[3] = v.tensor_copy(out=o_c[1][1][:], in_=ps_f[5][:, 0:257])
            for q_ in range(4):
                E.i("obfree", insts[q_])
            if debug and hh == 0 and jj == 0:
                E.w("sync", "obfree")
                for b_ in range(2):
                    for sl_ in range(2):
                        E.dma("sync", "stg0", dbg5[:, 2 * b_ + sl_, :], o_c[b_][sl_][:])
            inst = None
            if v:
                v.reciprocal(r2s[0][:], o_c[1][0][:, 256:257])
                v.reciprocal(r2s[1][:], o_c[1][1][:, 256:257])
                v.reciprocal(r1s[0][:], o_c[0][0][:, 256:257])
                v.reciprocal(r1s[1][:], o_c[0][1][:, 256:257])
                v.tensor_mul(r2l[0][:], r2s[0][:], lam_bc[:])
                v.tensor_mul(r2l[1][:], r2s[1][:], lam_bc[:])
                v.tensor_scalar(out=t2_sb[0][:], in0=o_c[1][0][:, 0:256],
                                scalar1=r2l[0][:], scalar2=None, op0=ALU.mult)
                v.tensor_scalar(out=t2_sb[1][:], in0=o_c[1][1][:, 0:256],
                                scalar1=r2l[1][:], scalar2=None, op0=ALU.mult)
                v.scalar_tensor_tensor(out=a_sb[0][:], in0=o_c[0][0][:, 0:256],
                                       scalar=r1s[0][:], in1=t2_sb[0][:],
                                       op0=ALU.mult, op1=ALU.subtract)
                inst = v.scalar_tensor_tensor(out=a_sb[1][:], in0=o_c[0][1][:, 0:256],
                                              scalar=r1s[1][:], in1=t2_sb[1][:],
                                              op0=ALU.mult, op1=ALU.subtract)
            E.i("oeread", inst)
            inst = None
            if v:
                v.tensor_mul(sqe[0][:], a_sb[0][:], a_sb[0][:])
                v.tensor_mul(sqe[1][:], a_sb[1][:], a_sb[1][:])
                v.reduce_sum(ssm[0][:], sqe[0][:], axis=AX.X)
                inst = v.reduce_sum(ssm[1][:], sqe[1][:], axis=AX.X)
            E.i("ssum", inst)
            pend_act.append((hh, jj, 16 * hh + jj, E.c["ssum"]))

        pend_epi = []
        hl_pend = []
        carry = None
        for h in range(H):
            slot = h % 2
            hsem = f"hl{slot}"
            q1, q2, k1, k2 = qk_buf[slot]
            vc = v_buf[slot]
            E.w("tensor", hsem)
            E.w("scalar", hsem)
            assert not hl_pend
            if h + 1 < H:
                hl_pend = head_load_items(h + 1)
            for jn, (kind, idx) in enumerate(pc_sched[h]):
                dslot = (h * 23 + jn) % 2
                E.w("gpsimd", f"dwc_st{dslot}")
                if kind == "dw":
                    E.dma("gpsimd", f"dwc_ld{dslot}", dwc_stage[dslot][:], dwT[idx])
                    E.w("gpsimd", f"dwc_ld{dslot}")
                    E.dma("gpsimd", f"dwc_st{dslot}", dw_bf[idx], dwc_stage[dslot][:])
                elif kind == "gw":
                    E.dma("gpsimd", f"dwc_ld{dslot}", dwc_stage[dslot][:, 0:D], gwT[idx])
                    E.w("gpsimd", f"dwc_ld{dslot}")
                    E.dma("gpsimd", f"dwc_st{dslot}", gw_bf[idx], dwc_stage[dslot][:, 0:D])
                else:
                    E.dma("gpsimd", f"dwc_ld{dslot}", dwc_stage[dslot][:, 0:D], uwT[idx])
                    E.w("gpsimd", f"dwc_ld{dslot}")
                    E.dma("gpsimd", f"dwc_st{dslot}", uw_bf[idx], dwc_stage[dslot][:, 0:D])

            for j in range(NTQ):
                nkt = 2 * j + 2
                for c in range(nkt):
                    bank = (0, 1, 6)[ktg[0] % 3]
                    ktg[0] += 1
                    scg = "scg0"
                    pslot = c % NPT
                    diag = c >= 2 * j
                    E.w("tensor", "pr", slack=2)
                    E.mm(ps_f[bank][:, 0:256], k1[:, c * P:(c + 1) * P],
                         q1[:, j * 256:(j + 1) * 256], start=True, stop=True)
                    E.mm(ps_f[bank][:, 256:512], k2[:, c * P:(c + 1) * P],
                         q2[:, j * 256:(j + 1) * 256], start=True, stop=True, sem=scg)
                    E.w("scalar", scg)
                    E.w("scalar", f"psf{pslot}")
                    inst = None
                    if s:
                        inst = s.activation(out=p_t[pslot][:], in_=ps_f[bank][:],
                                            func=AF.Exp, bias=expbias[:], scale=SCALE)
                    E.i(f"pr{pslot}", inst)
                    pr_tgt = E.c["pr"]
                    pm_tgt = None
                    if diag:
                        dsl = c - 2 * j
                        E.w("vector", f"pr{pslot}")
                        E.w("vector", "obfree")
                        inst = None
                        if v:
                            inst = v.tensor_mul(pmask_t[dsl][:], p_t[pslot][:],
                                                masks[:, dsl, :])
                        E.i(f"pm{pslot}", inst)
                        pm_tgt = E.c["pm"]
                        src_t = pmask_t[dsl]
                    else:
                        src_t = p_t[pslot]
                    if carry is not None:
                        emit_pv(carry)
                    carry = (c, src_t, pr_tgt, pm_tgt, nkt, vc)
                    if c == 0 and pend_epi:
                        emit_epilogue(pend_epi.pop(0))
                    if c == min(3, nkt - 1) and pend_act:
                        emit_act2(pend_act.pop(0))
                    if c == min(4, nkt - 1) and len(pend_tail) >= 2:
                        emit_tail2(pend_tail.pop(0))
                pend_epi.append((h, j))
                for _ in range(3):
                    if hl_pend:
                        sem_, out_, in_ = hl_pend.pop(0)
                        E.w("sync", "obfree")
                        E.dma("sync", sem_, out_, in_)

        emit_pv(carry)
        while pend_epi:
            emit_epilogue(pend_epi.pop(0))
        while pend_act:
            emit_act2(pend_act.pop(0))
        while pend_tail:
            emit_tail2(pend_tail.pop(0))

        phase_barrier()

        E.dma("gpsimd", "wo_ld", wo_sb[:], wo.rearrange("(c p) n -> p c n", p=P))
        E.w("vector", "wo_ld")
        inst = None
        if v:
            for c in range(NFC + 1):
                if c < NFC:
                    v.tensor_scalar(out=wo_stg[c % 2][:], in0=wo_sb[:, c, :],
                                    scalar1=w2b[:, c % 2:c % 2 + 1], scalar2=None,
                                    op0=ALU.mult)
                if c >= 1:
                    inst = v.tensor_copy(out=wo_sb[:, c - 1, :], in_=wo_stg[(c - 1) % 2][:])
        E.i("wo_sc", inst)
        E.w("tensor", "wo_sc")

        def ln_rows(rows, s_stat, s_rows, s_bc, s_app):
            m_r, vv_r, r_r, b_r = rows
            E.w("tensor", s_app)
            E.w("vector", s_stat)
            inst = None
            if v:
                v.tensor_scalar_mul(m_r[:], ps_f[2][0:1, :], 1.0 / D)
                v.tensor_mul(b_r[:], m_r[:], m_r[:])
                v.tensor_copy(out=dumv[:], in_=onesrow[0:1, 0:1])
                inst = v.scalar_tensor_tensor(out=vv_r[:], in0=ps_f[3][0:1, :],
                                              scalar=1.0 / D, in1=b_r[:],
                                              op0=ALU.mult, op1=ALU.subtract)
            E.i(s_rows, inst)
            E.w("scalar", s_rows)
            inst = None
            if s:
                s.activation(out=r_r[:], in_=vv_r[:], func=AF.Ln, bias=eps1[:], scale=1.0)
                inst = s.activation(out=r_r[:], in_=r_r[:], func=AF.Exp, bias=zero1[:],
                                    scale=-0.5)
            E.i(s_rows, inst)
            E.w("vector", s_rows)
            inst = None
            if v:
                inst = v.tensor_mul(b_r[:], m_r[:], r_r[:])
            E.i(s_rows, inst)
            E.w("tensor", s_rows)
            E.mm(ps_f[4][:], onesrow[:], r_r[:], start=True, stop=True)
            E.mm(ps_f[5][:], negrow[:], b_r[:], start=True, stop=True, sem=s_bc)
            E.w("vector", s_bc)

        for tcg in range(9):
            aslot = tcg % 2
            if tcg == 0:
                E.dma("sync", "at_ld0", at_t[0][:], a_scr[0])
            if tcg < 8:
                E.w("tensor", f"at_ld{aslot}")
            if tcg + 1 < 8:
                nslot = (tcg + 1) % 2
                E.w("sync", "mmf")
                E.dma("sync", f"at_ld{nslot}", at_t[nslot][:], a_scr[tcg + 1])
            if tcg >= 1:
                ln_rows((m_row, vv_row, r_row, b_row), "stat2", "rows", "bc2", "app")
            for fo in range(NFC + 2):
                if tcg < 8:
                    if fo < NFC:
                        bank = fo % 2
                        xslot = fo % 2
                        E.w("gpsimd", f"h1g{xslot}")
                        E.dma("gpsimd", f"xt_r{xslot}", xt_r[xslot][:],
                              xT[fo * P:(fo + 1) * P, tcg * 512:(tcg + 1) * 512])
                        E.w("tensor", "evf")
                        if fo == 0:
                            E.w("tensor", "rows")
                        last = None
                        for c in range(NFC):
                            last = E.mm(ps_f[bank][:], wo_sb[:, c, fo * P:(fo + 1) * P],
                                        at_t[aslot][:, c, :], start=(c == 0),
                                        stop=(c == NFC - 1))
                        E.i("mmf", last)
                    if fo >= 2:
                        f2 = fo - 2
                        E.w("tensor", "evf")
                        E.mm(ps_f[2][0:1, :], onescol[:], xrb[f2 % 2][:],
                             start=(f2 == 0), stop=(f2 == NFC - 1))
                        E.mm(ps_f[3][0:1, :], onescol[:], sq3[f2 % 2][:],
                             start=(f2 == 0), stop=(f2 == NFC - 1),
                             sem=("stat2" if f2 == NFC - 1 else None))
                    if 1 <= fo <= NFC:
                        f1 = fo - 1
                        minst = None
                        if v:
                            v.tensor_copy(out=xrb[f1 % 2][:], in_=xres[tcg % 2][:, f1, :])
                            minst = v.tensor_mul(sq3[f1 % 2][:], xres[tcg % 2][:, f1, :],
                                                 xres[tcg % 2][:, f1, :])
                        E.i("evf", minst)
                    if fo < NFC:
                        E.w("vector", "mmf")
                        E.w("vector", f"xt_r{xslot}")
                        ainst = None
                        if v:
                            ainst = v.tensor_add(xres[tcg % 2][:, fo, :], ps_f[bank][:],
                                                 xt_r[xslot][:])
                        E.i(f"h1g{xslot}", ainst)
                if tcg >= 1:
                    pc_ = tcg - 1
                    if fo < NFC:
                        if v:
                            v.tensor_mul(ln_tmp[fo % 2][:], xres[pc_ % 2][:, fo, :],
                                         ps_f[4][:])
                    if 1 <= fo <= NFC:
                        f1 = fo - 1
                        E.w("vector", f"y{f1 % 2}")
                        if v:
                            v.tensor_add(h1f_st[f1 % 2][:], ln_tmp[f1 % 2][:], ps_f[5][:])
                    if 2 <= fo:
                        f2 = fo - 2
                        hslot = f2 % 2
                        inst = None
                        if v:
                            inst = v.tensor_copy(out=h1b_st[hslot][:], in_=h1f_st[hslot][:])
                        E.i("app", inst)
                        E.w("gpsimd", "app")
                        E.dma("gpsimd", f"y{hslot}",
                              h1f_scr[f2][:, pc_ * 512:(pc_ + 1) * 512], h1f_st[hslot][:])
                        E.dma("gpsimd", f"y{hslot}",
                              h1b_scr[f2][:, pc_ * 512:(pc_ + 1) * 512], h1b_st[hslot][:])

        phase_barrier()

        for i_ in range(2):
            E.dma("sync", f"gw{i_}", gw_t[i_][:], gw_bf[i_])
            E.dma("sync", f"uw{i_}", uw_t[i_][:], uw_bf[i_])
        h1_pool = ["h1pl", "h1pl2", "hdn_ld0", "hdn_ld1",
                   "dwt0", "dwt1", "at_ld0", "at_ld1"]
        h1_tg = []
        h1r_ap = h1b_scr.rearrange("c p n -> p c n")
        for tc_ in range(8):
            sm = h1_pool[tc_]
            E.dma(("sync", "gpsimd")[tc_ % 2], sm,
                  h1_pl[:, :, tc_ * 512:(tc_ + 1) * 512],
                  h1r_ap[:, :, tc_ * 512:(tc_ + 1) * 512])
            h1_tg.append((sm, E.c[SEM_ALIAS.get(sm, sm)]))
        for i in range(NI):
            gslot = i % 4
            if i >= 2:
                E.w("sync", "gu_grp", slack=24)
                E.dma("sync", f"gw{gslot}", gw_t[gslot][:], gw_bf[i])
                E.dma("sync", f"uw{gslot}", uw_t[gslot][:], uw_bf[i])
            E.w("tensor", f"gw{gslot}")
            E.w("tensor", f"uw{gslot}")
            for tc in range(8):
                if i == 0:
                    E.wt("tensor", h1_tg[tc][0], h1_tg[tc][1])
                gi = i * 8 + tc
                gbank = gi % 2
                ubank = 2 + gi % 2
                E.w("tensor", "sg_done", slack=1)
                E.w("tensor", "uu_done", slack=1)
                for c in range(NFC):
                    E.mm(ps_f[gbank][:], gw_t[gslot][:, c, :],
                         h1_pl[:, c, tc * 512:(tc + 1) * 512],
                         start=(c == 0), stop=(c == NFC - 1))
                lastu = None
                for c in range(NFC):
                    lastu = E.mm(ps_f[ubank][:], uw_t[gslot][:, c, :],
                                 h1_pl[:, c, tc * 512:(tc + 1) * 512],
                                 start=(c == 0), stop=(c == NFC - 1))
                E.i("gu_grp", lastu)
                E.w("scalar", "gu_grp")
                E.w("scalar", "hdn_done")
                inst = None
                if s:
                    inst = s.activation(out=sg_sb[gi % 2][:], in_=ps_f[gbank][:],
                                        func=AF.Silu, bias=zeroc[:],
                                        scale=gs_sb[:, i:i + 1])
                E.i("sg_done", inst)
                E.w("vector", "gu_grp")
                inst = None
                if v:
                    inst = v.tensor_scalar(out=uu_sb[gi % 2][:], in0=ps_f[ubank][:],
                                           scalar1=us_sb[:, i:i + 1], scalar2=None,
                                           op0=ALU.mult)
                E.i("uu_done", inst)
                if gi >= 1:
                    pg = gi - 1
                    pi_, ptc = pg // 8, pg % 8
                    E.w("vector", "sg_done", slack=1)
                    E.w("vector", f"hdng{pg % 4}")
                    inst = None
                    if v:
                        inst = v.tensor_mul(hdn_st[pg % 4][:], sg_sb[pg % 2][:],
                                            uu_sb[pg % 2][:])
                    E.i("hdn_done", inst)
                    E.w("gpsimd", "hdn_done")
                    E.dma("gpsimd", f"hdng{pg % 4}", hdn_scr[ptc][:, pi_, :],
                          hdn_st[pg % 4][:])
        pg = NI * 8 - 1
        pi_, ptc = pg // 8, pg % 8
        E.w("vector", "sg_done")
        E.w("vector", f"hdng{pg % 4}")
        inst = None
        if v:
            inst = v.tensor_mul(hdn_st[pg % 4][:], sg_sb[pg % 2][:], uu_sb[pg % 2][:])
        E.i("hdn_done", inst)
        E.w("gpsimd", "hdn_done")
        E.dma("gpsimd", f"hdng{pg % 4}", hdn_scr[ptc][:, pi_, :], hdn_st[pg % 4][:])

        phase_barrier()

        for tcg in range(9):
            hdslot = tcg % 2
            if tcg == 0:
                E.dma("sync", "hdn_ld0", hdn_t[0][:], hdn_scr[0])
            if tcg < 8:
                E.w("tensor", f"hdn_ld{hdslot}")
            if tcg + 1 < 8:
                nslot = (tcg + 1) % 2
                E.w("sync", "mm4")
                E.dma("sync", f"hdn_ld{nslot}", hdn_t[nslot][:], hdn_scr[tcg + 1])
            if tcg >= 1:
                ln_rows((m_row, vv_row, r_row, b_row), "stat4", "rows4", "bc4", "app4")
            for ho in range(NFC + 2):
                if tcg < 8:
                    if ho < NFC:
                        bank = ho % 2
                        dslot = ho % 2
                        hrslot = ho % 2
                        E.w("sync", "mm4", slack=1)
                        E.dma("sync", f"dwt{dslot}", dw_t[dslot][:], dw_bf[ho])
                        E.w("gpsimd", "ev4")
                        E.dma("gpsimd", f"h1r{hrslot}", h1r[hrslot][:],
                              h1f_scr[ho][:, tcg * 512:(tcg + 1) * 512])
                        E.w("tensor", f"dwt{dslot}")
                        E.w("tensor", "ev4")
                        if ho == 0:
                            E.w("tensor", "rows4")
                        last = None
                        for c in range(NI):
                            last = E.mm(ps_f[bank][:], dw_t[dslot][:, c, :],
                                        hdn_t[hdslot][:, c, :], start=(c == 0),
                                        stop=(c == NI - 1))
                        E.i("mm4", last)
                    if ho >= 2:
                        h2 = ho - 2
                        E.w("tensor", "ev4")
                        E.mm(ps_f[2][0:1, :], onescol[:], xrb4[h2 % 2][:],
                             start=(h2 == 0), stop=(h2 == NFC - 1))
                        E.mm(ps_f[3][0:1, :], onescol[:], sq4[h2 % 2][:],
                             start=(h2 == 0), stop=(h2 == NFC - 1),
                             sem=("stat4" if h2 == NFC - 1 else None))
                    if 1 <= ho <= NFC:
                        h1_ = ho - 1
                        minst = None
                        if v:
                            v.tensor_copy(out=xrb4[h1_ % 2][:],
                                          in_=xres2[tcg % 2][:, h1_, :])
                            minst = v.tensor_mul(sq4[h1_ % 2][:],
                                                 xres2[tcg % 2][:, h1_, :],
                                                 xres2[tcg % 2][:, h1_, :])
                        E.i("ev4", minst)
                    if ho < NFC:
                        E.w("vector", "mm4")
                        E.w("vector", f"h1r{hrslot}")
                        if v:
                            v.scalar_tensor_tensor(out=xres2[tcg % 2][:, ho, :],
                                                   in0=ps_f[bank][:],
                                                   scalar=ds_sb[:, ho:ho + 1],
                                                   in1=h1r[hrslot][:],
                                                   op0=ALU.mult, op1=ALU.add)
                if tcg >= 1 and ho <= NFC:
                    pc_ = tcg - 1
                    if ho < NFC:
                        if v:
                            v.tensor_mul(ln_tmp4[ho % 2][:], xres2[pc_ % 2][:, ho, :],
                                         ps_f[4][:])
                    if ho >= 1:
                        po = ho - 1
                        yslot = po % 2
                        E.w("vector", f"y{yslot}")
                        inst = None
                        if v:
                            inst = v.tensor_add(y_st[yslot][:], ln_tmp4[po % 2][:],
                                                ps_f[5][:])
                        E.i("app4", inst)
                        E.w("gpsimd", "app4")
                        E.dma("gpsimd", f"y{yslot}",
                              yT[po * P:(po + 1) * P, pc_ * 512:(pc_ + 1) * 512],
                              y_st[yslot][:])

        for sem in ("ast", "stg0", "stg1", "stg2", "stg3",
                    "h1pl", "hl0", "hl1", "at_ld0", "at_ld1", "hdn_ld0", "hdn_ld1",
                    "dwt0", "dwt1", "gw0", "gw1", "gw2s", "gw3s",
                    "uw0", "uw1", "uw2s", "uw3s"):
            E.w("sync", sem)
        for sem in ("y0", "y1", "y2", "y3", "dwc_st0", "dwc_st1",
                    "hdng0", "hdng1", "hdng2", "hdng3"):
            E.w("gpsimd", sem)
        for sem in ("xt_ld", "w_ld0", "w_ld1", "wv_ld0", "wv_ld1", "dwc_ld0", "dwc_ld1",
                    "wo_ld", "xt_r0", "xt_r1", "gw0", "gw1", "uw0", "uw1",
                    "h1r0", "h1r1"):
            E.w("gpsimd", sem)
        phase_barrier()

    with nc.Block() as block:
        base_counts = {n: 0 for n in SEM_NAMES}
        passes = {}
        for en in ENGINES:
            cc = dict(base_counts)

            def body(eng, cc=cc, en=en):
                emit_all(Em(nc, en, eng, sems, cc))
            getattr(block, en)(body)
            passes[en] = cc
        ref = passes["sync"]
        for n2, cc in passes.items():
            assert cc == ref, f"pass {n2} counts diverge: " + str(
                {k: (cc[k], ref[k]) for k in cc if cc[k] != ref[k]})

    es.close()
    return nc


_CACHE = {}


def _get_nc(debug=False):
    if debug not in _CACHE:
        _CACHE[debug] = build_nc(debug)
    return _CACHE[debug]


def make_in_maps(inputs):
    x = np.asarray(inputs["L0_attn_x"], dtype=np.float32)
    def tile_w(w, m, nchunks=NFC):
        w = np.asarray(w, dtype=np.float32)
        no = w.shape[1] // m
        return np.ascontiguousarray(w.reshape(nchunks, P, no, m).transpose(2, 1, 0, 3))

    wq = tile_w(inputs["L0_attn_Wq"], P)
    wk = tile_w(inputs["L0_attn_Wk"], P)
    wv = tile_w(inputs["L0_attn_Wv"], 512)
    wo = np.ascontiguousarray(np.asarray(inputs["L0_attn_Wo"], dtype=np.float32))
    lamv = np.concatenate([
        np.asarray(inputs["L0_attn_lambda_q1"], dtype=np.float32),
        np.asarray(inputs["L0_attn_lambda_k1"], dtype=np.float32),
        np.asarray(inputs["L0_attn_lambda_q2"], dtype=np.float32),
        np.asarray(inputs["L0_attn_lambda_k2"], dtype=np.float32),
    ]).reshape(1, 512)
    subw2 = np.ascontiguousarray(
        np.asarray(inputs["L0_attn_subln_weight"], dtype=np.float32).reshape(2, P).T)
    gwT = tile_w(np.asarray(inputs["L0_ffn_gate_w"], dtype=np.float32).T, P)
    uwT = tile_w(np.asarray(inputs["L0_ffn_up_w"], dtype=np.float32).T, P)
    dwT = tile_w(np.asarray(inputs["L0_ffn_down_w"], dtype=np.float32).T, P, nchunks=NI)
    gs_t = np.ascontiguousarray(
        np.asarray(inputs["L0_ffn_gate_s"], dtype=np.float32).reshape(NI, P).T)
    us_t = np.ascontiguousarray(
        np.asarray(inputs["L0_ffn_up_s"], dtype=np.float32).reshape(NI, P).T)
    ds_t = np.ascontiguousarray(
        np.asarray(inputs["L0_ffn_down_s"], dtype=np.float32).reshape(NFC, P).T)
    in_maps = []
    for c in range(x.shape[0]):
        xTc = np.ascontiguousarray(x[c].T)
        in_maps.append({
            "xT": xTc, "wq": wq, "wk": wk, "wv": wv, "wo": wo,
            "gwT": gwT, "uwT": uwT, "dwT": dwT,
            "lam": lamv, "subw2": subw2, "gs_t": gs_t, "us_t": us_t, "ds_t": ds_t,
        })
    return in_maps


def kernel(**inputs):
    nc = _get_nc(False)
    in_maps = make_in_maps(inputs)
    res = run_bass_kernel_spmd(nc, in_maps, core_ids=list(range(8)))
    out = np.stack([np.asarray(res.results[c]["yT"]).T for c in range(8)])
    return np.ascontiguousarray(out)


# version 25
# speedup vs baseline: 1.0240x; 1.0240x over previous
import math
import numpy as np
import ml_dtypes
import concourse.bass as bass
import concourse.mybir as mybir
from concourse.bass_utils import run_bass_kernel_spmd

F32 = mybir.dt.float32
BF16 = mybir.dt.bfloat16
AF = mybir.ActivationFunctionType
ALU = mybir.AluOpType
AX = mybir.AxisListType

P = 128
S = 4096
D = 2048
H = 8
DH = 128
NFC = D // P
NI = 5632 // P
NTQ = S // 256
SCALE = DH ** -0.5
LAMBDA_INIT = 0.8 - 0.6 * math.exp(-0.3 * 12)
EPS = 1e-5
NPT = 4

ENGINES = ("sync", "gpsimd", "tensor", "vector", "scalar")

SEM_NAMES = [
    "bar",
    "xt_ld", "w_ld0", "w_ld1", "wv_ld0", "wv_ld1", "wv_ld2",
    "grp", "sq_done", "statmm", "red",
    "stg0", "stg1", "stg2", "stg3", "stgr",
    "vmax", "mhat", "lamf", "bc", "prep",
    "hl0", "hl1",
    "scg", "pr", "pm",
    "psf0", "psf1", "psf2", "psf3",
    "pvall", "obfree", "oeread", "ssum", "rstd", "an", "tp", "atcp", "ast",
    "dwc_ld0", "dwc_ld1", "dwc_st0", "dwc_st1",
    "wo_ld", "wo_sc", "at_ld0", "at_ld1", "xt_r0", "xt_r1",
    "mmf", "evf", "stat2", "rows", "bc2", "app",
    "h1g0", "h1g1",
    "h1pl", "h1pl2", "gw0", "gw1", "gw2s", "gw3s", "uw0", "uw1", "uw2s", "uw3s", "gu_grp", "sg_done",
    "uu_done", "hdn_done",
    "hdng0", "hdng1", "hdng2", "hdng3",
    "hdn_ld0", "hdn_ld1", "dwt0", "dwt1", "dwt2s", "h1r0", "h1r1",
    "mm4", "ev4", "stat4", "rows4", "bc4", "app4",
    "y0", "y1", "y2", "y3",
]

SEM_ALIAS = {}
for _b in ("scg",):
    SEM_ALIAS[_b + "0"] = _b
    SEM_ALIAS[_b + "1"] = _b
SEM_ALIAS["gw2"] = "gw2s"
SEM_ALIAS["uw2"] = "uw2s"
SEM_ALIAS["gw3"] = "gw3s"
SEM_ALIAS["uw3"] = "uw3s"
SEM_ALIAS["dwt2"] = "dwt2s"
for _b in ("pr", "pm", "stgr"):
    for _i in range(4):
        SEM_ALIAS[_b + str(_i)] = _b

_DT_SIZE = {F32: 4, BF16: 2}


def _ceil32(x):
    return (x + 31) & ~31


class Arena:
    def __init__(self, nc, base, top):
        self.nc, self.base, self.top, self.cur = nc, base, top, base
        self.uid = 0

    def reset(self):
        self.cur = self.base

    def alloc(self, name, shape, dtype):
        nb = _DT_SIZE[dtype]
        for s_ in shape[1:]:
            nb *= s_
        t = self.nc.alloc_sbuf_tensor_at(f"{name}_{self.uid}", list(shape), dtype,
                                         offset=self.cur)
        self.uid += 1
        self.cur += _ceil32(nb)
        assert self.cur <= self.top, f"arena overflow at {name}: {self.cur} > {self.top}"
        return t


class Em:
    def __init__(self, nc, name, eng, sems, counts):
        self.nc, self.name, self.eng, self.sems, self.c = nc, name, eng, sems, counts
        self.dma_used = {}

    def i(self, sem, inst=None, k=1):
        sem = SEM_ALIAS.get(sem, sem)
        self.c[sem] += k
        if inst is not None:
            inst.then_inc(self.sems[sem], k)

    def di(self, ename, sem, k=1):
        sem = SEM_ALIAS.get(sem, sem)
        self.c[sem] += k
        if self.name == ename:
            self.eng.sem_inc(self.sems[sem], k)

    def wt(self, ename, sem, target):
        sem = SEM_ALIAS.get(sem, sem)
        if self.name == ename and target > 0:
            self.eng.wait_ge(self.sems[sem], target)

    def w(self, ename, sem, slack=0):
        sem = SEM_ALIAS.get(sem, sem)
        if self.name == ename:
            tgt = self.c[sem] - slack
            if tgt > 0:
                self.eng.wait_ge(self.sems[sem], tgt)

    def dma(self, qname, sem, out, in_, **kw):
        inst = None
        if self.name == qname:
            inst = self.eng.dma_start(out=out, in_=in_, **kw)
        self.dma_used[SEM_ALIAS.get(sem, sem)] = qname
        self.i(sem, inst, 16)

    def mm(self, out, lhsT, rhs, start, stop, sem=None):
        inst = None
        if self.name == "tensor":
            inst = self.eng.matmul(out, lhsT, rhs, start=start, stop=stop)
        if sem is not None:
            self.i(sem, inst)
        return inst




def build_nc(debug=False):
    nc = bass.Bass("TRN2", target_bir_lowering=False, debug=False,
                   detect_race_conditions=False)
    scratch_kind = "ExternalOutput" if debug else "Internal"

    xT = nc.dram_tensor("xT", [D, S], F32, kind="ExternalInput").ap()
    wq = nc.dram_tensor("wq", [16, P, NFC, P], F32, kind="ExternalInput").ap()
    wk = nc.dram_tensor("wk", [16, P, NFC, P], F32, kind="ExternalInput").ap()
    wv = nc.dram_tensor("wv", [4, P, NFC, 512], F32, kind="ExternalInput").ap()
    wo = nc.dram_tensor("wo", [D, D], F32, kind="ExternalInput").ap()
    gwT = nc.dram_tensor("gwT", [NI, P, NFC, P], F32, kind="ExternalInput").ap()
    uwT = nc.dram_tensor("uwT", [NI, P, NFC, P], F32, kind="ExternalInput").ap()
    dwT = nc.dram_tensor("dwT", [NFC, P, NI, P], F32, kind="ExternalInput").ap()
    lam = nc.dram_tensor("lam", [1, 512], F32, kind="ExternalInput").ap()
    subw2 = nc.dram_tensor("subw2", [P, 2], F32, kind="ExternalInput").ap()
    gs_t = nc.dram_tensor("gs_t", [P, NI], F32, kind="ExternalInput").ap()
    us_t = nc.dram_tensor("us_t", [P, NI], F32, kind="ExternalInput").ap()
    ds_t = nc.dram_tensor("ds_t", [P, NFC], F32, kind="ExternalInput").ap()
    yT = nc.dram_tensor("yT", [D, S], F32, kind="ExternalOutput").ap()

    q_scr = nc.dram_tensor("q_scr", [16, P, S], BF16, kind=scratch_kind).ap()
    k_scr = nc.dram_tensor("k_scr", [16, P, S], BF16, kind=scratch_kind).ap()
    v_scr = nc.dram_tensor("v_scr", [4, 32, P, 512], BF16, kind=scratch_kind).ap()
    a_scr = nc.dram_tensor("a_scr", [8, P, 16, 512], BF16, kind=scratch_kind).ap()
    h1f_scr = nc.dram_tensor("h1f_scr", [NFC, P, S], F32, kind=scratch_kind).ap()
    h1b_scr = nc.dram_tensor("h1b_scr", [NFC, P, S], BF16, kind="Internal").ap()
    hdn_scr = nc.dram_tensor("hdn_scr", [8, P, NI, 512], BF16, kind=scratch_kind).ap()
    dw_bf = nc.dram_tensor("dw_bf", [NFC, P, NI, P], BF16, kind="Internal").ap()
    gw_bf = nc.dram_tensor("gw_bf", [NI, P, NFC * P], BF16, kind="Internal").ap()
    uw_bf = nc.dram_tensor("uw_bf", [NI, P, NFC * P], BF16, kind="Internal").ap()
    dbg = nc.dram_tensor("dbg", [P, 64], F32, kind=scratch_kind).ap() if debug else None
    dbg2 = nc.dram_tensor("dbg2", [P, 2, 512], F32, kind=scratch_kind).ap() if debug else None
    dbg3 = nc.dram_tensor("dbg3", [P, 32, 257], F32, kind=scratch_kind).ap() if debug else None
    dbg4 = nc.dram_tensor("dbg4", [P, 2, 512], F32, kind=scratch_kind).ap() if debug else None
    dbg5 = nc.dram_tensor("dbg5", [P, 4, 257], F32, kind=scratch_kind).ap() if debug else None
    dbg6 = nc.dram_tensor("dbg6", [P, 2, 256], F32, kind=scratch_kind).ap() if debug else None

    ident_h = nc.inline_tensor(np.eye(P, dtype=ml_dtypes.bfloat16), name="identc_i")
    onesc_h = nc.inline_tensor(np.ones((P, 1), dtype=ml_dtypes.bfloat16), name="onescol_i")
    onesr_h = nc.inline_tensor(np.ones((1, P), dtype=np.float32), name="onesrow_i")
    negr_h = nc.inline_tensor(np.full((1, P), -1.0, dtype=np.float32), name="negrow_i")
    m_np = np.zeros((P, 2, 512), dtype=ml_dtypes.bfloat16)
    for d_ in range(2):
        pat = ((np.arange(P)[:, None] + 128 * d_) <= np.arange(256)[None, :])
        m_np[:, d_, 0:256] = pat.astype(ml_dtypes.bfloat16)
        m_np[:, d_, 256:512] = pat.astype(ml_dtypes.bfloat16)
    mask_h = nc.inline_tensor(m_np, name="maskc_i")

    ident = nc.alloc_sbuf_tensor("ident", [P, P], BF16)
    onescol = nc.alloc_sbuf_tensor("onescol", [P, 1], BF16)
    onesrow = nc.alloc_sbuf_tensor("onesrow", [1, P], F32)
    negrow = nc.alloc_sbuf_tensor("negrow", [1, P], F32)
    masks = nc.alloc_sbuf_tensor("masks", [P, 2, 512], BF16)
    lam_sb = nc.alloc_sbuf_tensor("lam_sb", [1, 512], F32)
    w2_sb = nc.alloc_sbuf_tensor("w2_sb", [P, 2], F32)
    gs_sb = nc.alloc_sbuf_tensor("gs_sb", [P, NI], F32)
    us_sb = nc.alloc_sbuf_tensor("us_sb", [P, NI], F32)
    ds_sb = nc.alloc_sbuf_tensor("ds_sb", [P, NFC], F32)
    zeroc = nc.alloc_sbuf_tensor("zeroc", [P, 1], F32)
    epsc = nc.alloc_sbuf_tensor("epsc", [P, 1], F32)
    zero1 = nc.alloc_sbuf_tensor("zero1", [1, 1], F32)
    eps1 = nc.alloc_sbuf_tensor("eps1", [1, 1], F32)
    c0r = nc.alloc_sbuf_tensor("c0r", [1, 1], F32)
    m60 = nc.alloc_sbuf_tensor("m60", [1, 1], F32)
    nmt = nc.alloc_sbuf_tensor("nmt", [1, 256], F32)
    pmax = nc.alloc_sbuf_tensor("pmax", [1, 32], F32)
    prod = nc.alloc_sbuf_tensor("prod", [1, 16], F32)
    lamp = nc.alloc_sbuf_tensor("lamp", [1, 256], F32)
    w2b = nc.alloc_sbuf_tensor("w2b", [P, 2], F32)
    tmp11 = nc.alloc_sbuf_tensor("tmp11", [1, 8], F32)
    expbias = nc.alloc_sbuf_tensor("expbias", [P, 1], F32)
    lam_bc = nc.alloc_sbuf_tensor("lam_bc", [P, 1], F32)
    dumv = nc.alloc_sbuf_tensor("dumv", [1, 1], F32)
    dums = nc.alloc_sbuf_tensor("dums", [1, 1], F32)
    m_row = nc.alloc_sbuf_tensor("m_row", [1, 512], F32)
    vv_row = nc.alloc_sbuf_tensor("vv_row", [1, 512], F32)
    r_row = nc.alloc_sbuf_tensor("r_row", [1, 512], F32)
    b_row = nc.alloc_sbuf_tensor("b_row", [1, 512], F32)

    ar = Arena(nc, _ceil32(nc.sbuf_base), nc.sbuf_top)

    ar.reset()
    xt_bf = ar.alloc("xt_bf", [P, NFC, S], BF16)
    wqk_t = [ar.alloc("wqk_t", [P, NFC, P], BF16) for _ in range(2)]
    wv_t = [ar.alloc("wv_t", [P, NFC, 512], BF16) for _ in range(2)]
    stg = [ar.alloc("stg", [P, 512], BF16) for _ in range(4)]
    sqb = [ar.alloc("sqb", [P, 512], BF16) for _ in range(2)]

    ar.reset()
    qk_buf = [[ar.alloc("qk", [P, S], BF16) for _ in range(4)] for _ in range(2)]
    v_buf = [ar.alloc("vbuf", [P, 32, 257], BF16) for _ in range(2)]
    p_t = [ar.alloc("pt", [P, 512], BF16) for _ in range(NPT)]
    pmask_t = [ar.alloc("pmt", [P, 512], BF16) for _ in range(2)]
    o_c = [[ar.alloc("oc", [P, 257], F32) for _ in range(2)] for _ in range(2)]
    t2_sb = [ar.alloc("t2s", [P, 256], F32) for _ in range(2)]
    a_sb = [ar.alloc("a_sb", [P, 256], F32) for _ in range(2)]
    sqe = [ar.alloc("sqe", [P, 256], F32) for _ in range(2)]
    r1s = [ar.alloc("r1s", [P, 1], F32) for _ in range(2)]
    r2s = [ar.alloc("r2s", [P, 1], F32) for _ in range(2)]
    r2l = [ar.alloc("r2l", [P, 1], F32) for _ in range(2)]
    ssm = [ar.alloc("ssm", [P, 1], F32) for _ in range(2)]
    rst = [ar.alloc("rst", [P, 1], F32) for _ in range(2)]
    an_sb = [[ar.alloc("an", [P, 256], BF16) for _ in range(2)] for _ in range(3)]
    at_stage = [ar.alloc("at_st", [P, 512], BF16) for _ in range(2)]
    dwc_stage = [ar.alloc("dwc", [P, NI * P], BF16) for _ in range(2)]

    ar.reset()
    wo_sb = ar.alloc("wo_sb", [P, NFC, D], BF16)
    at_t = [ar.alloc("at_t", [P, NFC, 512], BF16) for _ in range(2)]
    xt_r = [ar.alloc("xt_r", [P, 512], F32) for _ in range(2)]
    xres = [ar.alloc("xres", [P, NFC, 512], F32) for _ in range(2)]
    xrb = [ar.alloc("xrb", [P, 512], BF16) for _ in range(2)]
    sq3 = [ar.alloc("sq3", [P, 512], BF16) for _ in range(2)]
    ln_tmp = [ar.alloc("ln_tmp", [P, 512], F32) for _ in range(2)]
    wo_stg = [ar.alloc("wo_stg", [P, D], BF16) for _ in range(2)]
    h1f_st = [ar.alloc("h1f_st", [P, 512], F32) for _ in range(2)]
    h1b_st = [ar.alloc("h1b_st", [P, 512], BF16) for _ in range(2)]

    ar.reset()
    h1_pl = ar.alloc("h1_pl", [P, NFC, S], BF16)
    gw_t = [ar.alloc("gw_t", [P, NFC, P], BF16) for _ in range(4)]
    uw_t = [ar.alloc("uw_t", [P, NFC, P], BF16) for _ in range(4)]
    sg_sb = [ar.alloc("sg", [P, 512], F32) for _ in range(2)]
    uu_sb = [ar.alloc("uu", [P, 512], F32) for _ in range(2)]
    hdn_st = [ar.alloc("hdn_st", [P, 512], BF16) for _ in range(4)]

    ar.reset()
    hdn_t = [ar.alloc("hdn_t", [P, NI, 512], BF16) for _ in range(2)]
    dw_t = [ar.alloc("dw_t", [P, NI, P], BF16) for _ in range(2)]
    h1r = [ar.alloc("h1r", [P, 512], F32) for _ in range(2)]
    xres2 = [ar.alloc("xres2", [P, NFC, 512], F32) for _ in range(2)]
    xrb4 = [ar.alloc("xrb4", [P, 512], BF16) for _ in range(2)]
    sq4 = [ar.alloc("sq4", [P, 512], BF16) for _ in range(2)]
    ln_tmp4 = [ar.alloc("ln_tmp4", [P, 512], F32) for _ in range(2)]
    y_st = [ar.alloc("y_st", [P, 512], F32) for _ in range(2)]

    ps_f = [nc.alloc_psum_tensor(f"psf{i}", [P, 512], F32) for i in range(7)]
    ps_t = nc.alloc_psum_tensor("pst", [P, 512], BF16)

    import contextlib
    es = contextlib.ExitStack()
    sems = {n: es.enter_context(nc.semaphore(n)) for n in SEM_NAMES}
    C0 = math.log(1.02 * SCALE)

    def emit_all(E: Em):
        t = E.eng if E.name == "tensor" else None
        v = E.eng if E.name == "vector" else None
        s = E.eng if E.name == "scalar" else None

        def phase_barrier():
            for sem in sorted(E.dma_used):
                E.w(E.name, sem)
            inst = None
            if t:
                inst = t.matmul(ps_f[6][0:1, 0:1], onesrow[0:1, 0:1],
                                onesrow[0:1, 0:1], start=True, stop=True)
            E.i("bar", inst)
            inst = None
            if v:
                inst = v.tensor_copy(out=dumv[:], in_=onesrow[0:1, 0:1])
            E.i("bar", inst)
            inst = None
            if s:
                inst = s.activation(out=dums[:], in_=onesrow[0:1, 0:1], func=AF.Copy,
                                    bias=0.0, scale=1.0)
            E.i("bar", inst)
            E.eng.wait_ge(E.sems["bar"], E.c["bar"])

        E.dma("sync", "stg0", ident[:], ident_h.ap())
        E.dma("sync", "stg0", onescol[:], onesc_h.ap())
        E.dma("sync", "stg0", onesrow[:], onesr_h.ap())
        E.dma("sync", "stg0", negrow[:], negr_h.ap())
        E.dma("sync", "stg0", masks[:], mask_h.ap())
        E.dma("sync", "stg0", lam_sb[:], lam)
        E.dma("sync", "stg0", w2_sb[:], subw2)
        E.dma("sync", "stg0", gs_sb[:], gs_t)
        E.dma("sync", "stg0", us_sb[:], us_t)
        E.dma("sync", "stg0", ds_sb[:], ds_t)
        if v:
            v.memset(zeroc[:], 0.0)
            v.memset(epsc[:], EPS)
            v.memset(zero1[:], 0.0)
            v.memset(eps1[:], EPS)
            v.memset(c0r[:], C0)
            v.memset(m60[:], -60.0)
            inst = v.memset(nmt[:], 0.0)
        else:
            inst = None
        E.i("red", inst)
        E.w("vector", "stg0")
        if v:
            inst = v.tensor_scalar_mul(w2b[:], w2_sb[:], 1.0 - LAMBDA_INIT)
        else:
            inst = None
        E.i("red", inst)
        E.w("scalar", "red")
        E.w("tensor", "stg0")
        E.dma("gpsimd", "w_ld0", wqk_t[0][:], wq[0])
        E.dma("gpsimd", "w_ld1", wqk_t[1][:], wq[1])
        xr_ap = xT.rearrange("(c p) n -> p c n", p=P)
        xt_pool = ["xt_ld", "hl0", "hl1", "at_ld0",
                   "at_ld1", "hdn_ld0", "hdn_ld1", "dwt0"]
        xt_tg = []
        for tc_ in range(8):
            sm = xt_pool[tc_]
            E.dma("gpsimd", sm, xt_bf[:, :, tc_ * 512:(tc_ + 1) * 512],
                  xr_ap[:, :, tc_ * 512:(tc_ + 1) * 512])
            xt_tg.append((sm, E.c[SEM_ALIAS.get(sm, sm)]))

        grp = 0
        for pi, (w_ap, scr) in enumerate([(wq, q_scr), (wk, k_scr)]):
            for oc in range(16):
                li = pi * 16 + oc
                slot = li % 2
                wsem = f"w_ld{slot}"
                if li >= 2:
                    E.w("gpsimd", "grp", slack=8)
                    E.dma("gpsimd", wsem, wqk_t[slot][:], w_ap[oc])
                E.w("tensor", wsem)
                for tc in range(8):
                    if li == 0:
                        E.wt("tensor", xt_tg[tc][0], xt_tg[tc][1])
                    bank = grp % 2
                    E.w("tensor", "sq_done", slack=1)
                    last = None
                    for c in range(NFC):
                        last = E.mm(ps_f[bank][:], wqk_t[slot][:, c, :],
                                    xt_bf[:, c, tc * 512:(tc + 1) * 512],
                                    start=(c == 0), stop=(c == NFC - 1))
                    E.i("grp", last)
                    if grp > 0:
                        E.w("tensor", "sq_done")
                        E.w("tensor", "red")
                        E.mm(ps_f[2][0:1, :], onescol[:], sqb[(grp - 1) % 2][:],
                             start=True, stop=True, sem="statmm")
                    E.w("scalar", "grp")
                    stslot = grp % 4
                    E.w("scalar", f"stg{stslot}")
                    cinst = None
                    if s:
                        cinst = s.activation(out=stg[stslot][:], in_=ps_f[bank][:],
                                             func=AF.Copy, bias=0.0, scale=1.0)
                    E.i(f"stgr{stslot}", cinst)
                    E.w("scalar", "statmm")
                    inst = None
                    if s:
                        inst = s.activation(out=sqb[grp % 2][:], in_=ps_f[bank][:],
                                            func=AF.Square, bias=zeroc[:], scale=1.0)
                    E.i("sq_done", inst)
                    if grp > 0:
                        E.w("vector", "statmm")
                        inst = None
                        if v:
                            inst = v.reduce_max(nmt[0:1, grp - 1:grp], ps_f[2][0:1, :],
                                                axis=AX.X)
                        E.i("red", inst)
                    E.w("sync", f"stgr{stslot}")
                    E.dma("sync", f"stg{stslot}", scr[oc][:, tc * 512:(tc + 1) * 512],
                          stg[stslot][:])
                    grp += 1
        E.w("tensor", "sq_done")
        E.w("tensor", "red")
        E.mm(ps_f[2][0:1, :], onescol[:], sqb[(grp - 1) % 2][:], start=True, stop=True,
             sem="statmm")
        E.w("vector", "statmm")
        inst = None
        if v:
            inst = v.reduce_max(nmt[0:1, grp - 1:grp], ps_f[2][0:1, :], axis=AX.X)
        E.i("red", inst)

        if v:
            for pl in range(32):
                v.reduce_max(pmax[0:1, pl:pl + 1], nmt[0:1, pl * 8:(pl + 1) * 8],
                             axis=AX.X)
            for pl in range(16):
                v.tensor_mul(prod[0:1, pl:pl + 1], pmax[0:1, pl:pl + 1],
                             pmax[0:1, 16 + pl:17 + pl])
            v.tensor_mul(lamp[0:1, 0:P], lam_sb[0:1, 0:P], lam_sb[0:1, P:2 * P])
            v.tensor_mul(lamp[0:1, P:2 * P], lam_sb[0:1, 2 * P:3 * P],
                         lam_sb[0:1, 3 * P:4 * P])
            v.reduce_max(tmp11[0:1, 1:2], prod[0:1, 0:16], axis=AX.X)
            v.reduce_sum(tmp11[0:1, 2:3], lamp[0:1, 0:P], axis=AX.X)
            inst = v.reduce_sum(tmp11[0:1, 3:4], lamp[0:1, P:2 * P], axis=AX.X)
        else:
            inst = None
        E.i("vmax", inst)
        E.w("scalar", "vmax")
        inst = None
        if s:
            s.activation(out=tmp11[0:1, 1:2], in_=tmp11[0:1, 1:2], func=AF.Ln,
                         bias=zero1[:], scale=1.0)
            s.activation(out=tmp11[0:1, 1:2], in_=tmp11[0:1, 1:2], func=AF.Exp,
                         bias=c0r[:], scale=0.5)
            s.activation(out=tmp11[0:1, 1:2], in_=tmp11[0:1, 1:2], func=AF.Relu,
                         bias=m60[:], scale=1.0)
            s.activation(out=tmp11[0:1, 2:3], in_=tmp11[0:1, 2:3], func=AF.Exp,
                         bias=zero1[:], scale=1.0)
            inst = s.activation(out=tmp11[0:1, 3:4], in_=tmp11[0:1, 3:4], func=AF.Exp,
                                bias=zero1[:], scale=1.0)
        E.i("mhat", inst)
        E.w("vector", "mhat")
        inst = None
        if v:
            v.tensor_sub(tmp11[0:1, 6:7], tmp11[0:1, 2:3], tmp11[0:1, 3:4])
            v.tensor_scalar_mul(tmp11[0:1, 1:2], tmp11[0:1, 1:2], -1.0)
            inst = v.tensor_scalar_add(tmp11[0:1, 6:7], tmp11[0:1, 6:7], LAMBDA_INIT)
        E.i("lamf", inst)
        E.w("tensor", "lamf")
        E.mm(ps_f[6][:, 0:1], onesrow[:], tmp11[0:1, 1:2], start=True, stop=True)
        E.mm(ps_f[6][:, 1:2], onesrow[:], tmp11[0:1, 6:7], start=True, stop=True, sem="bc")
        E.w("vector", "bc")
        inst = None
        if v:
            v.tensor_copy(out=expbias[:], in_=ps_f[6][:, 0:1])
            inst = v.tensor_copy(out=lam_bc[:], in_=ps_f[6][:, 1:2])
        E.i("prep", inst)
        if debug:
            E.w("sync", "prep")
            E.dma("sync", "stg0", dbg[:, 0:1], expbias[:], allow_slow_non_contiguous=True)
            E.dma("sync", "stg0", dbg[:, 1:2], lam_bc[:], allow_slow_non_contiguous=True)
            E.dma("gpsimd", "xt_ld", dbg2[:], masks[:])

        E.w("tensor", "sq_done")
        vg = 0
        for hp in range(4):
            slot = hp % 2
            wsem = f"wv_ld{slot}"
            E.w("gpsimd", "grp", slack=32)
            E.dma("gpsimd", wsem, wv_t[slot][:], wv[hp])
            E.w("tensor", wsem)
            for tc in range(32):
                bank = vg % 2
                E.w("tensor", "stgr", slack=1)
                last = None
                for c in range(NFC):
                    last = E.mm(ps_f[bank][:], xt_bf[:, c, tc * P:(tc + 1) * P],
                                wv_t[slot][:, c, :], start=(c == 0), stop=(c == NFC - 1))
                E.i("grp", last)
                E.w("scalar", "grp")
                stslot = vg % 4
                E.w("scalar", f"stg{stslot}")
                inst = None
                if s:
                    inst = s.activation(out=stg[stslot][:], in_=ps_f[bank][:], func=AF.Copy,
                                        bias=0.0, scale=1.0)
                E.i(f"stgr{stslot}", inst)
                E.w("sync", f"stgr{stslot}")
                E.dma("sync", f"stg{stslot}", v_scr[hp, tc][:], stg[stslot][:])
                vg += 1

        phase_barrier()

        inst = None
        if v:
            v.memset(v_buf[0][:, :, 256:257], 1.0)
            inst = v.memset(v_buf[1][:, :, 256:257], 1.0)
        E.i("prep", inst)
        E.w("scalar", "prep")
        E.w("tensor", "prep")

        def head_load_items(h):
            slot = h % 2
            sem = f"hl{slot}"
            items = []
            for comp in range(2):
                items.append((sem, qk_buf[slot][comp][:], q_scr[2 * h + comp]))
                items.append((sem, qk_buf[slot][2 + comp][:], k_scr[2 * h + comp]))
            off = (h % 2) * 256
            for tcv in range(32):
                items.append((sem, v_buf[slot][:, tcv, 0:256],
                              v_scr[h // 2, tcv][:, off:off + 256]))
            return items

        def head_load(h):
            E.w("sync", "obfree")
            for sem, out, in_ in head_load_items(h):
                E.dma("sync", sem, out, in_)

        pend_tail = []
        pend_act = []

        def emit_act2(st):
            hh, jj, jg_, ssum_tgt = st
            E.wt("scalar", "ssum", ssum_tgt)
            inst = None
            if s:
                s.activation(out=rst[0][:], in_=ssm[0][:], func=AF.Ln, bias=epsc[:],
                             scale=1.0 / 256.0)
                s.activation(out=rst[1][:], in_=ssm[1][:], func=AF.Ln, bias=epsc[:],
                             scale=1.0 / 256.0)
                s.activation(out=rst[0][:], in_=rst[0][:], func=AF.Exp,
                             bias=zeroc[:], scale=-0.5)
                inst = s.activation(out=rst[1][:], in_=rst[1][:], func=AF.Exp,
                                    bias=zeroc[:], scale=-0.5)
            E.i("rstd", inst)
            E.w("vector", "rstd")
            E.w("vector", "tp")
            inst = None
            if v:
                v.tensor_scalar(out=an_sb[jg_ % 3][0][:], in0=a_sb[0][:],
                                scalar1=rst[0][:], scalar2=None, op0=ALU.mult)
                inst = v.tensor_scalar(out=an_sb[jg_ % 3][1][:], in0=a_sb[1][:],
                                       scalar1=rst[1][:], scalar2=None, op0=ALU.mult)
            E.i("an", inst)
            pend_tail.append((hh, jj, jg_, E.c["an"]))

        pc_jobs = ([("dw", ho_) for ho_ in range(NFC)]
                   + [("gw", i_) for i_ in range(NI)]
                   + [("uw", i_) for i_ in range(NI)])
        pc_sched = {h_: pc_jobs[(len(pc_jobs) * h_) // H:(len(pc_jobs) * (h_ + 1)) // H]
                    for h_ in range(H)}

        def emit_tail2(st):
            hh, jj, jg_, an_tgt = st
            E.wt("tensor", "an", an_tgt)
            E.w("tensor", "atcp")
            for sl in range(2):
                last = None
                if t:
                    t.transpose(ps_t[:, sl * P:(sl + 1) * P],
                                an_sb[jg_ % 3][sl][:, 0:P], ident[:])
                    last = t.transpose(ps_t[:, 256 + sl * P:256 + (sl + 1) * P],
                                       an_sb[jg_ % 3][sl][:, P:2 * P], ident[:])
                E.i("tp", last)
            E.w("vector", "tp")
            E.w("vector", "ast")
            inst = None
            if v:
                inst = v.tensor_copy(out=at_stage[jj % 2][:], in_=ps_t[:])
            E.i("atcp", inst)
            E.w("sync", "atcp")
            tj, oj = (jj * 256) // 512, (jj * 256) % 512
            E.dma("sync", "ast", a_scr[tj][:, 2 * hh, oj:oj + 256],
                  at_stage[jj % 2][:, 0:256])
            E.dma("sync", "ast", a_scr[tj][:, 2 * hh + 1, oj:oj + 256],
                  at_stage[jj % 2][:, 256:512])

        ktg = [0]
        head_load(0)
        if debug:
            E.w("gpsimd", "hl0")
            E.dma("gpsimd", "xt_ld", dbg3[:], v_buf[0][:])

        def emit_pv(st):
            c_, srct_, prt_, pmt_, nkt_, vc_ = st
            if pmt_ is not None:
                E.wt("tensor", "pm", pmt_)
            else:
                E.wt("tensor", "pr", prt_)
            if c_ == 0:
                E.w("tensor", "obfree")
            last = None
            for b_ in range(2):
                for sl in range(2):
                    last = E.mm(ps_f[2 + 2 * b_ + sl][:, 0:257],
                                srct_[:, b_ * 256 + sl * P:b_ * 256 + (sl + 1) * P],
                                vc_[:, c_, :], start=(c_ == 0),
                                stop=(c_ == nkt_ - 1))
            E.i(f"psf{c_ % NPT}", last)

        def emit_epilogue(st):
            hh, jj = st
            nkt_ = 2 * jj + 2
            lastslot = (nkt_ - 1) % NPT
            E.w("vector", f"psf{lastslot}")
            insts = [None, None, None, None]
            if v:
                insts[0] = v.tensor_copy(out=o_c[0][0][:], in_=ps_f[2][:, 0:257])
                insts[1] = v.tensor_copy(out=o_c[0][1][:], in_=ps_f[3][:, 0:257])
                insts[2] = v.tensor_copy(out=o_c[1][0][:], in_=ps_f[4][:, 0:257])
                insts[3] = v.tensor_copy(out=o_c[1][1][:], in_=ps_f[5][:, 0:257])
            for q_ in range(4):
                E.i("obfree", insts[q_])
            if debug and hh == 0 and jj == 0:
                E.w("sync", "obfree")
                for b_ in range(2):
                    for sl_ in range(2):
                        E.dma("sync", "stg0", dbg5[:, 2 * b_ + sl_, :], o_c[b_][sl_][:])
            inst = None
            if v:
                v.reciprocal(r2s[0][:], o_c[1][0][:, 256:257])
                v.reciprocal(r2s[1][:], o_c[1][1][:, 256:257])
                v.reciprocal(r1s[0][:], o_c[0][0][:, 256:257])
                v.reciprocal(r1s[1][:], o_c[0][1][:, 256:257])
                v.tensor_mul(r2l[0][:], r2s[0][:], lam_bc[:])
                v.tensor_mul(r2l[1][:], r2s[1][:], lam_bc[:])
                v.tensor_scalar(out=t2_sb[0][:], in0=o_c[1][0][:, 0:256],
                                scalar1=r2l[0][:], scalar2=None, op0=ALU.mult)
                v.tensor_scalar(out=t2_sb[1][:], in0=o_c[1][1][:, 0:256],
                                scalar1=r2l[1][:], scalar2=None, op0=ALU.mult)
                v.scalar_tensor_tensor(out=a_sb[0][:], in0=o_c[0][0][:, 0:256],
                                       scalar=r1s[0][:], in1=t2_sb[0][:],
                                       op0=ALU.mult, op1=ALU.subtract)
                inst = v.scalar_tensor_tensor(out=a_sb[1][:], in0=o_c[0][1][:, 0:256],
                                              scalar=r1s[1][:], in1=t2_sb[1][:],
                                              op0=ALU.mult, op1=ALU.subtract)
            E.i("oeread", inst)
            inst = None
            if v:
                v.tensor_mul(sqe[0][:], a_sb[0][:], a_sb[0][:])
                v.tensor_mul(sqe[1][:], a_sb[1][:], a_sb[1][:])
                v.reduce_sum(ssm[0][:], sqe[0][:], axis=AX.X)
                inst = v.reduce_sum(ssm[1][:], sqe[1][:], axis=AX.X)
            E.i("ssum", inst)
            pend_act.append((hh, jj, 16 * hh + jj, E.c["ssum"]))

        pend_epi = []
        hl_pend = []
        carries = []
        for h in range(H):
            slot = h % 2
            hsem = f"hl{slot}"
            q1, q2, k1, k2 = qk_buf[slot]
            vc = v_buf[slot]
            E.w("tensor", hsem)
            E.w("scalar", hsem)
            assert not hl_pend
            if h + 1 < H:
                hl_pend = head_load_items(h + 1)
            for jn, (kind, idx) in enumerate(pc_sched[h]):
                dslot = (h * 23 + jn) % 2
                E.w("gpsimd", f"dwc_st{dslot}")
                if kind == "dw":
                    E.dma("gpsimd", f"dwc_ld{dslot}", dwc_stage[dslot][:], dwT[idx])
                    E.w("gpsimd", f"dwc_ld{dslot}")
                    E.dma("gpsimd", f"dwc_st{dslot}", dw_bf[idx], dwc_stage[dslot][:])
                elif kind == "gw":
                    E.dma("gpsimd", f"dwc_ld{dslot}", dwc_stage[dslot][:, 0:D], gwT[idx])
                    E.w("gpsimd", f"dwc_ld{dslot}")
                    E.dma("gpsimd", f"dwc_st{dslot}", gw_bf[idx], dwc_stage[dslot][:, 0:D])
                else:
                    E.dma("gpsimd", f"dwc_ld{dslot}", dwc_stage[dslot][:, 0:D], uwT[idx])
                    E.w("gpsimd", f"dwc_ld{dslot}")
                    E.dma("gpsimd", f"dwc_st{dslot}", uw_bf[idx], dwc_stage[dslot][:, 0:D])

            for j in range(NTQ):
                nkt = 2 * j + 2
                for c in range(nkt):
                    bank = (0, 1, 6)[ktg[0] % 3]
                    ktg[0] += 1
                    scg = "scg0"
                    pslot = c % NPT
                    diag = c >= 2 * j
                    if len(carries) >= 2:
                        emit_pv(carries.pop(0))
                    E.w("tensor", "pr", slack=2)
                    E.mm(ps_f[bank][:, 0:256], k1[:, c * P:(c + 1) * P],
                         q1[:, j * 256:(j + 1) * 256], start=True, stop=True)
                    E.mm(ps_f[bank][:, 256:512], k2[:, c * P:(c + 1) * P],
                         q2[:, j * 256:(j + 1) * 256], start=True, stop=True, sem=scg)
                    E.w("scalar", scg)
                    E.w("scalar", f"psf{pslot}")
                    inst = None
                    if s:
                        inst = s.activation(out=p_t[pslot][:], in_=ps_f[bank][:],
                                            func=AF.Exp, bias=expbias[:], scale=SCALE)
                    E.i(f"pr{pslot}", inst)
                    pr_tgt = E.c["pr"]
                    pm_tgt = None
                    if diag:
                        dsl = c - 2 * j
                        E.w("vector", f"pr{pslot}")
                        E.w("vector", "obfree")
                        inst = None
                        if v:
                            inst = v.tensor_mul(pmask_t[dsl][:], p_t[pslot][:],
                                                masks[:, dsl, :])
                        E.i(f"pm{pslot}", inst)
                        pm_tgt = E.c["pm"]
                        src_t = pmask_t[dsl]
                    else:
                        src_t = p_t[pslot]
                    carries.append((c, src_t, pr_tgt, pm_tgt, nkt, vc))
                    if c == 1 and pend_epi:
                        emit_epilogue(pend_epi.pop(0))
                    if c == min(3, nkt - 1) and pend_act:
                        emit_act2(pend_act.pop(0))
                    if c == min(4, nkt - 1) and len(pend_tail) >= 2:
                        emit_tail2(pend_tail.pop(0))
                pend_epi.append((h, j))
                for _ in range(3):
                    if hl_pend:
                        sem_, out_, in_ = hl_pend.pop(0)
                        E.w("sync", "obfree")
                        E.dma("sync", sem_, out_, in_)

        while carries:
            emit_pv(carries.pop(0))
        while pend_epi:
            emit_epilogue(pend_epi.pop(0))
        while pend_act:
            emit_act2(pend_act.pop(0))
        while pend_tail:
            emit_tail2(pend_tail.pop(0))

        phase_barrier()

        E.dma("gpsimd", "wo_ld", wo_sb[:], wo.rearrange("(c p) n -> p c n", p=P))
        E.w("vector", "wo_ld")
        inst = None
        if v:
            for c in range(NFC + 1):
                if c < NFC:
                    v.tensor_scalar(out=wo_stg[c % 2][:], in0=wo_sb[:, c, :],
                                    scalar1=w2b[:, c % 2:c % 2 + 1], scalar2=None,
                                    op0=ALU.mult)
                if c >= 1:
                    inst = v.tensor_copy(out=wo_sb[:, c - 1, :], in_=wo_stg[(c - 1) % 2][:])
        E.i("wo_sc", inst)
        E.w("tensor", "wo_sc")

        def ln_rows(rows, s_stat, s_rows, s_bc, s_app):
            m_r, vv_r, r_r, b_r = rows
            E.w("tensor", s_app)
            E.w("vector", s_stat)
            inst = None
            if v:
                v.tensor_scalar_mul(m_r[:], ps_f[2][0:1, :], 1.0 / D)
                v.tensor_mul(b_r[:], m_r[:], m_r[:])
                v.tensor_copy(out=dumv[:], in_=onesrow[0:1, 0:1])
                inst = v.scalar_tensor_tensor(out=vv_r[:], in0=ps_f[3][0:1, :],
                                              scalar=1.0 / D, in1=b_r[:],
                                              op0=ALU.mult, op1=ALU.subtract)
            E.i(s_rows, inst)
            E.w("scalar", s_rows)
            inst = None
            if s:
                s.activation(out=r_r[:], in_=vv_r[:], func=AF.Ln, bias=eps1[:], scale=1.0)
                inst = s.activation(out=r_r[:], in_=r_r[:], func=AF.Exp, bias=zero1[:],
                                    scale=-0.5)
            E.i(s_rows, inst)
            E.w("vector", s_rows)
            inst = None
            if v:
                inst = v.tensor_mul(b_r[:], m_r[:], r_r[:])
            E.i(s_rows, inst)
            E.w("tensor", s_rows)
            E.mm(ps_f[4][:], onesrow[:], r_r[:], start=True, stop=True)
            E.mm(ps_f[5][:], negrow[:], b_r[:], start=True, stop=True, sem=s_bc)
            E.w("vector", s_bc)

        for tcg in range(9):
            aslot = tcg % 2
            if tcg == 0:
                E.dma("sync", "at_ld0", at_t[0][:], a_scr[0])
            if tcg < 8:
                E.w("tensor", f"at_ld{aslot}")
            if tcg + 1 < 8:
                nslot = (tcg + 1) % 2
                E.w("sync", "mmf")
                E.dma("sync", f"at_ld{nslot}", at_t[nslot][:], a_scr[tcg + 1])
            if tcg >= 1:
                ln_rows((m_row, vv_row, r_row, b_row), "stat2", "rows", "bc2", "app")
            for fo in range(NFC + 2):
                if tcg < 8:
                    if fo < NFC:
                        bank = fo % 2
                        xslot = fo % 2
                        E.w("gpsimd", f"h1g{xslot}")
                        E.dma("gpsimd", f"xt_r{xslot}", xt_r[xslot][:],
                              xT[fo * P:(fo + 1) * P, tcg * 512:(tcg + 1) * 512])
                        E.w("tensor", "evf")
                        if fo == 0:
                            E.w("tensor", "rows")
                        last = None
                        for c in range(NFC):
                            last = E.mm(ps_f[bank][:], wo_sb[:, c, fo * P:(fo + 1) * P],
                                        at_t[aslot][:, c, :], start=(c == 0),
                                        stop=(c == NFC - 1))
                        E.i("mmf", last)
                    if fo >= 2:
                        f2 = fo - 2
                        E.w("tensor", "evf")
                        E.mm(ps_f[2][0:1, :], onescol[:], xrb[f2 % 2][:],
                             start=(f2 == 0), stop=(f2 == NFC - 1))
                        E.mm(ps_f[3][0:1, :], onescol[:], sq3[f2 % 2][:],
                             start=(f2 == 0), stop=(f2 == NFC - 1),
                             sem=("stat2" if f2 == NFC - 1 else None))
                    if 1 <= fo <= NFC:
                        f1 = fo - 1
                        minst = None
                        if v:
                            v.tensor_copy(out=xrb[f1 % 2][:], in_=xres[tcg % 2][:, f1, :])
                            minst = v.tensor_mul(sq3[f1 % 2][:], xres[tcg % 2][:, f1, :],
                                                 xres[tcg % 2][:, f1, :])
                        E.i("evf", minst)
                    if fo < NFC:
                        E.w("vector", "mmf")
                        E.w("vector", f"xt_r{xslot}")
                        ainst = None
                        if v:
                            ainst = v.tensor_add(xres[tcg % 2][:, fo, :], ps_f[bank][:],
                                                 xt_r[xslot][:])
                        E.i(f"h1g{xslot}", ainst)
                if tcg >= 1:
                    pc_ = tcg - 1
                    if fo < NFC:
                        if v:
                            v.tensor_mul(ln_tmp[fo % 2][:], xres[pc_ % 2][:, fo, :],
                                         ps_f[4][:])
                    if 1 <= fo <= NFC:
                        f1 = fo - 1
                        E.w("vector", f"y{f1 % 2}")
                        if v:
                            v.tensor_add(h1f_st[f1 % 2][:], ln_tmp[f1 % 2][:], ps_f[5][:])
                    if 2 <= fo:
                        f2 = fo - 2
                        hslot = f2 % 2
                        inst = None
                        if v:
                            inst = v.tensor_copy(out=h1b_st[hslot][:], in_=h1f_st[hslot][:])
                        E.i("app", inst)
                        E.w("gpsimd", "app")
                        E.dma("gpsimd", f"y{hslot}",
                              h1f_scr[f2][:, pc_ * 512:(pc_ + 1) * 512], h1f_st[hslot][:])
                        E.dma("gpsimd", f"y{hslot}",
                              h1b_scr[f2][:, pc_ * 512:(pc_ + 1) * 512], h1b_st[hslot][:])

        phase_barrier()

        for i_ in range(2):
            E.dma("sync", f"gw{i_}", gw_t[i_][:], gw_bf[i_])
            E.dma("sync", f"uw{i_}", uw_t[i_][:], uw_bf[i_])
        h1_pool = ["h1pl", "h1pl2", "hdn_ld0", "hdn_ld1",
                   "dwt0", "dwt1", "at_ld0", "at_ld1"]
        h1_tg = []
        h1r_ap = h1b_scr.rearrange("c p n -> p c n")
        for tc_ in range(8):
            sm = h1_pool[tc_]
            E.dma(("sync", "gpsimd")[tc_ % 2], sm,
                  h1_pl[:, :, tc_ * 512:(tc_ + 1) * 512],
                  h1r_ap[:, :, tc_ * 512:(tc_ + 1) * 512])
            h1_tg.append((sm, E.c[SEM_ALIAS.get(sm, sm)]))
        for i in range(NI):
            gslot = i % 4
            if i >= 2:
                E.w("sync", "gu_grp", slack=24)
                E.dma("sync", f"gw{gslot}", gw_t[gslot][:], gw_bf[i])
                E.dma("sync", f"uw{gslot}", uw_t[gslot][:], uw_bf[i])
            E.w("tensor", f"gw{gslot}")
            E.w("tensor", f"uw{gslot}")
            for tc in range(8):
                if i == 0:
                    E.wt("tensor", h1_tg[tc][0], h1_tg[tc][1])
                gi = i * 8 + tc
                gbank = gi % 2
                ubank = 2 + gi % 2
                E.w("tensor", "sg_done", slack=1)
                E.w("tensor", "uu_done", slack=1)
                for c in range(NFC):
                    E.mm(ps_f[gbank][:], gw_t[gslot][:, c, :],
                         h1_pl[:, c, tc * 512:(tc + 1) * 512],
                         start=(c == 0), stop=(c == NFC - 1))
                lastu = None
                for c in range(NFC):
                    lastu = E.mm(ps_f[ubank][:], uw_t[gslot][:, c, :],
                                 h1_pl[:, c, tc * 512:(tc + 1) * 512],
                                 start=(c == 0), stop=(c == NFC - 1))
                E.i("gu_grp", lastu)
                E.w("scalar", "gu_grp")
                E.w("scalar", "hdn_done")
                inst = None
                if s:
                    inst = s.activation(out=sg_sb[gi % 2][:], in_=ps_f[gbank][:],
                                        func=AF.Silu, bias=zeroc[:],
                                        scale=gs_sb[:, i:i + 1])
                E.i("sg_done", inst)
                E.w("vector", "gu_grp")
                inst = None
                if v:
                    inst = v.tensor_scalar(out=uu_sb[gi % 2][:], in0=ps_f[ubank][:],
                                           scalar1=us_sb[:, i:i + 1], scalar2=None,
                                           op0=ALU.mult)
                E.i("uu_done", inst)
                if gi >= 1:
                    pg = gi - 1
                    pi_, ptc = pg // 8, pg % 8
                    E.w("vector", "sg_done", slack=1)
                    E.w("vector", f"hdng{pg % 4}")
                    inst = None
                    if v:
                        inst = v.tensor_mul(hdn_st[pg % 4][:], sg_sb[pg % 2][:],
                                            uu_sb[pg % 2][:])
                    E.i("hdn_done", inst)
                    E.w("gpsimd", "hdn_done")
                    E.dma("gpsimd", f"hdng{pg % 4}", hdn_scr[ptc][:, pi_, :],
                          hdn_st[pg % 4][:])
        pg = NI * 8 - 1
        pi_, ptc = pg // 8, pg % 8
        E.w("vector", "sg_done")
        E.w("vector", f"hdng{pg % 4}")
        inst = None
        if v:
            inst = v.tensor_mul(hdn_st[pg % 4][:], sg_sb[pg % 2][:], uu_sb[pg % 2][:])
        E.i("hdn_done", inst)
        E.w("gpsimd", "hdn_done")
        E.dma("gpsimd", f"hdng{pg % 4}", hdn_scr[ptc][:, pi_, :], hdn_st[pg % 4][:])

        phase_barrier()

        for tcg in range(9):
            hdslot = tcg % 2
            if tcg == 0:
                E.dma("sync", "hdn_ld0", hdn_t[0][:], hdn_scr[0])
            if tcg < 8:
                E.w("tensor", f"hdn_ld{hdslot}")
            if tcg + 1 < 8:
                nslot = (tcg + 1) % 2
                E.w("sync", "mm4")
                E.dma("sync", f"hdn_ld{nslot}", hdn_t[nslot][:], hdn_scr[tcg + 1])
            if tcg >= 1:
                ln_rows((m_row, vv_row, r_row, b_row), "stat4", "rows4", "bc4", "app4")
            for ho in range(NFC + 2):
                if tcg < 8:
                    if ho < NFC:
                        bank = ho % 2
                        dslot = ho % 2
                        hrslot = ho % 2
                        E.w("sync", "mm4", slack=1)
                        E.dma("sync", f"dwt{dslot}", dw_t[dslot][:], dw_bf[ho])
                        E.w("gpsimd", "ev4")
                        E.dma("gpsimd", f"h1r{hrslot}", h1r[hrslot][:],
                              h1f_scr[ho][:, tcg * 512:(tcg + 1) * 512])
                        E.w("tensor", f"dwt{dslot}")
                        E.w("tensor", "ev4")
                        if ho == 0:
                            E.w("tensor", "rows4")
                        last = None
                        for c in range(NI):
                            last = E.mm(ps_f[bank][:], dw_t[dslot][:, c, :],
                                        hdn_t[hdslot][:, c, :], start=(c == 0),
                                        stop=(c == NI - 1))
                        E.i("mm4", last)
                    if ho >= 2:
                        h2 = ho - 2
                        E.w("tensor", "ev4")
                        E.mm(ps_f[2][0:1, :], onescol[:], xrb4[h2 % 2][:],
                             start=(h2 == 0), stop=(h2 == NFC - 1))
                        E.mm(ps_f[3][0:1, :], onescol[:], sq4[h2 % 2][:],
                             start=(h2 == 0), stop=(h2 == NFC - 1),
                             sem=("stat4" if h2 == NFC - 1 else None))
                    if 1 <= ho <= NFC:
                        h1_ = ho - 1
                        minst = None
                        if v:
                            v.tensor_copy(out=xrb4[h1_ % 2][:],
                                          in_=xres2[tcg % 2][:, h1_, :])
                            minst = v.tensor_mul(sq4[h1_ % 2][:],
                                                 xres2[tcg % 2][:, h1_, :],
                                                 xres2[tcg % 2][:, h1_, :])
                        E.i("ev4", minst)
                    if ho < NFC:
                        E.w("vector", "mm4")
                        E.w("vector", f"h1r{hrslot}")
                        if v:
                            v.scalar_tensor_tensor(out=xres2[tcg % 2][:, ho, :],
                                                   in0=ps_f[bank][:],
                                                   scalar=ds_sb[:, ho:ho + 1],
                                                   in1=h1r[hrslot][:],
                                                   op0=ALU.mult, op1=ALU.add)
                if tcg >= 1 and ho <= NFC:
                    pc_ = tcg - 1
                    if ho < NFC:
                        if v:
                            v.tensor_mul(ln_tmp4[ho % 2][:], xres2[pc_ % 2][:, ho, :],
                                         ps_f[4][:])
                    if ho >= 1:
                        po = ho - 1
                        yslot = po % 2
                        E.w("vector", f"y{yslot}")
                        inst = None
                        if v:
                            inst = v.tensor_add(y_st[yslot][:], ln_tmp4[po % 2][:],
                                                ps_f[5][:])
                        E.i("app4", inst)
                        E.w("gpsimd", "app4")
                        E.dma("gpsimd", f"y{yslot}",
                              yT[po * P:(po + 1) * P, pc_ * 512:(pc_ + 1) * 512],
                              y_st[yslot][:])

        for sem in ("ast", "stg0", "stg1", "stg2", "stg3",
                    "h1pl", "hl0", "hl1", "at_ld0", "at_ld1", "hdn_ld0", "hdn_ld1",
                    "dwt0", "dwt1", "gw0", "gw1", "gw2s", "gw3s",
                    "uw0", "uw1", "uw2s", "uw3s"):
            E.w("sync", sem)
        for sem in ("y0", "y1", "y2", "y3", "dwc_st0", "dwc_st1",
                    "hdng0", "hdng1", "hdng2", "hdng3"):
            E.w("gpsimd", sem)
        for sem in ("xt_ld", "w_ld0", "w_ld1", "wv_ld0", "wv_ld1", "dwc_ld0", "dwc_ld1",
                    "wo_ld", "xt_r0", "xt_r1", "gw0", "gw1", "uw0", "uw1",
                    "h1r0", "h1r1"):
            E.w("gpsimd", sem)
        phase_barrier()

    with nc.Block() as block:
        base_counts = {n: 0 for n in SEM_NAMES}
        passes = {}
        for en in ENGINES:
            cc = dict(base_counts)

            def body(eng, cc=cc, en=en):
                emit_all(Em(nc, en, eng, sems, cc))
            getattr(block, en)(body)
            passes[en] = cc
        ref = passes["sync"]
        for n2, cc in passes.items():
            assert cc == ref, f"pass {n2} counts diverge: " + str(
                {k: (cc[k], ref[k]) for k in cc if cc[k] != ref[k]})

    es.close()
    return nc


_CACHE = {}


def _get_nc(debug=False):
    if debug not in _CACHE:
        _CACHE[debug] = build_nc(debug)
    return _CACHE[debug]


def make_in_maps(inputs):
    x = np.asarray(inputs["L0_attn_x"], dtype=np.float32)
    def tile_w(w, m, nchunks=NFC):
        w = np.asarray(w, dtype=np.float32)
        no = w.shape[1] // m
        return np.ascontiguousarray(w.reshape(nchunks, P, no, m).transpose(2, 1, 0, 3))

    wq = tile_w(inputs["L0_attn_Wq"], P)
    wk = tile_w(inputs["L0_attn_Wk"], P)
    wv = tile_w(inputs["L0_attn_Wv"], 512)
    wo = np.ascontiguousarray(np.asarray(inputs["L0_attn_Wo"], dtype=np.float32))
    lamv = np.concatenate([
        np.asarray(inputs["L0_attn_lambda_q1"], dtype=np.float32),
        np.asarray(inputs["L0_attn_lambda_k1"], dtype=np.float32),
        np.asarray(inputs["L0_attn_lambda_q2"], dtype=np.float32),
        np.asarray(inputs["L0_attn_lambda_k2"], dtype=np.float32),
    ]).reshape(1, 512)
    subw2 = np.ascontiguousarray(
        np.asarray(inputs["L0_attn_subln_weight"], dtype=np.float32).reshape(2, P).T)
    gwT = tile_w(np.asarray(inputs["L0_ffn_gate_w"], dtype=np.float32).T, P)
    uwT = tile_w(np.asarray(inputs["L0_ffn_up_w"], dtype=np.float32).T, P)
    dwT = tile_w(np.asarray(inputs["L0_ffn_down_w"], dtype=np.float32).T, P, nchunks=NI)
    gs_t = np.ascontiguousarray(
        np.asarray(inputs["L0_ffn_gate_s"], dtype=np.float32).reshape(NI, P).T)
    us_t = np.ascontiguousarray(
        np.asarray(inputs["L0_ffn_up_s"], dtype=np.float32).reshape(NI, P).T)
    ds_t = np.ascontiguousarray(
        np.asarray(inputs["L0_ffn_down_s"], dtype=np.float32).reshape(NFC, P).T)
    in_maps = []
    for c in range(x.shape[0]):
        xTc = np.ascontiguousarray(x[c].T)
        in_maps.append({
            "xT": xTc, "wq": wq, "wk": wk, "wv": wv, "wo": wo,
            "gwT": gwT, "uwT": uwT, "dwT": dwT,
            "lam": lamv, "subw2": subw2, "gs_t": gs_t, "us_t": us_t, "ds_t": ds_t,
        })
    return in_maps


def kernel(**inputs):
    nc = _get_nc(False)
    in_maps = make_in_maps(inputs)
    res = run_bass_kernel_spmd(nc, in_maps, core_ids=list(range(8)))
    out = np.stack([np.asarray(res.results[c]["yT"]).T for c in range(8)])
    return np.ascontiguousarray(out)


# version 26
# speedup vs baseline: 1.0626x; 1.0377x over previous
import math
import numpy as np
import ml_dtypes
import concourse.bass as bass
import concourse.mybir as mybir
from concourse.bass_utils import run_bass_kernel_spmd

F32 = mybir.dt.float32
BF16 = mybir.dt.bfloat16
AF = mybir.ActivationFunctionType
ALU = mybir.AluOpType
AX = mybir.AxisListType

P = 128
S = 4096
D = 2048
H = 8
DH = 128
NFC = D // P
NI = 5632 // P
NTQ = S // 256
SCALE = DH ** -0.5
LAMBDA_INIT = 0.8 - 0.6 * math.exp(-0.3 * 12)
EPS = 1e-5
NPT = 4

ENGINES = ("sync", "gpsimd", "tensor", "vector", "scalar")

SEM_NAMES = [
    "bar",
    "xt_ld", "w_ld0", "w_ld1", "wv_ld0", "wv_ld1", "wv_ld2",
    "grp", "sq_done", "statmm", "red",
    "stg0", "stg1", "stg2", "stg3", "stgr",
    "vmax", "mhat", "lamf", "bc", "prep",
    "hl0", "hl1",
    "scg", "pr", "pm",
    "psf0", "psf1", "psf2", "psf3",
    "pvall", "obfree", "oeread", "ssum", "rstd", "an", "tp", "atcp", "ast",
    "dwc_ld0", "dwc_ld1", "dwc_st0", "dwc_st1",
    "wo_ld", "wo_sc", "at_ld0", "at_ld1", "xt_r0", "xt_r1",
    "mmf", "evf", "stat2", "rows", "bc2", "app",
    "h1g0", "h1g1",
    "h1pl", "h1pl2", "gw0", "gw1", "gw2s", "gw3s", "uw0", "uw1", "uw2s", "uw3s", "gu_grp", "sg_done",
    "uu_done", "hdn_done",
    "hdng0", "hdng1", "hdng2", "hdng3",
    "hdn_ld0", "hdn_ld1", "dwt0", "dwt1", "dwt2s", "h1r0", "h1r1",
    "mm4", "ev4", "stat4", "rows4", "bc4", "app4",
    "y0", "y1", "y2", "y3",
]

SEM_ALIAS = {}
for _b in ("scg",):
    SEM_ALIAS[_b + "0"] = _b
    SEM_ALIAS[_b + "1"] = _b
SEM_ALIAS["gw2"] = "gw2s"
SEM_ALIAS["uw2"] = "uw2s"
SEM_ALIAS["gw3"] = "gw3s"
SEM_ALIAS["uw3"] = "uw3s"
SEM_ALIAS["dwt2"] = "dwt2s"
for _b in ("pr", "pm", "stgr"):
    for _i in range(4):
        SEM_ALIAS[_b + str(_i)] = _b

_DT_SIZE = {F32: 4, BF16: 2}


def _ceil32(x):
    return (x + 31) & ~31


class Arena:
    def __init__(self, nc, base, top):
        self.nc, self.base, self.top, self.cur = nc, base, top, base
        self.uid = 0

    def reset(self):
        self.cur = self.base

    def alloc(self, name, shape, dtype):
        nb = _DT_SIZE[dtype]
        for s_ in shape[1:]:
            nb *= s_
        t = self.nc.alloc_sbuf_tensor_at(f"{name}_{self.uid}", list(shape), dtype,
                                         offset=self.cur)
        self.uid += 1
        self.cur += _ceil32(nb)
        assert self.cur <= self.top, f"arena overflow at {name}: {self.cur} > {self.top}"
        return t


class Em:
    def __init__(self, nc, name, eng, sems, counts):
        self.nc, self.name, self.eng, self.sems, self.c = nc, name, eng, sems, counts
        self.dma_used = {}

    def i(self, sem, inst=None, k=1):
        sem = SEM_ALIAS.get(sem, sem)
        self.c[sem] += k
        if inst is not None:
            inst.then_inc(self.sems[sem], k)

    def di(self, ename, sem, k=1):
        sem = SEM_ALIAS.get(sem, sem)
        self.c[sem] += k
        if self.name == ename:
            self.eng.sem_inc(self.sems[sem], k)

    def wt(self, ename, sem, target):
        sem = SEM_ALIAS.get(sem, sem)
        if self.name == ename and target > 0:
            self.eng.wait_ge(self.sems[sem], target)

    def w(self, ename, sem, slack=0):
        sem = SEM_ALIAS.get(sem, sem)
        if self.name == ename:
            tgt = self.c[sem] - slack
            if tgt > 0:
                self.eng.wait_ge(self.sems[sem], tgt)

    def dma(self, qname, sem, out, in_, **kw):
        inst = None
        if self.name == qname:
            inst = self.eng.dma_start(out=out, in_=in_, **kw)
        self.dma_used[SEM_ALIAS.get(sem, sem)] = qname
        self.i(sem, inst, 16)

    def mm(self, out, lhsT, rhs, start, stop, sem=None):
        inst = None
        if self.name == "tensor":
            inst = self.eng.matmul(out, lhsT, rhs, start=start, stop=stop)
        if sem is not None:
            self.i(sem, inst)
        return inst




def build_nc(debug=False):
    nc = bass.Bass("TRN2", target_bir_lowering=False, debug=False,
                   detect_race_conditions=False)
    scratch_kind = "ExternalOutput" if debug else "Internal"

    xT = nc.dram_tensor("xT", [D, S], F32, kind="ExternalInput").ap()
    wq = nc.dram_tensor("wq", [16, P, NFC, P], F32, kind="ExternalInput").ap()
    wk = nc.dram_tensor("wk", [16, P, NFC, P], F32, kind="ExternalInput").ap()
    wv = nc.dram_tensor("wv", [4, P, NFC, 512], F32, kind="ExternalInput").ap()
    wo = nc.dram_tensor("wo", [D, D], F32, kind="ExternalInput").ap()
    gwT = nc.dram_tensor("gwT", [NI, P, NFC, P], F32, kind="ExternalInput").ap()
    uwT = nc.dram_tensor("uwT", [NI, P, NFC, P], F32, kind="ExternalInput").ap()
    dwT = nc.dram_tensor("dwT", [NFC, P, NI, P], F32, kind="ExternalInput").ap()
    lam = nc.dram_tensor("lam", [1, 512], F32, kind="ExternalInput").ap()
    subw2 = nc.dram_tensor("subw2", [P, 2], F32, kind="ExternalInput").ap()
    gs_t = nc.dram_tensor("gs_t", [P, NI], F32, kind="ExternalInput").ap()
    us_t = nc.dram_tensor("us_t", [P, NI], F32, kind="ExternalInput").ap()
    ds_t = nc.dram_tensor("ds_t", [P, NFC], F32, kind="ExternalInput").ap()
    yT = nc.dram_tensor("yT", [D, S], F32, kind="ExternalOutput").ap()

    q_scr = nc.dram_tensor("q_scr", [16, P, S], BF16, kind=scratch_kind).ap()
    k_scr = nc.dram_tensor("k_scr", [16, P, S], BF16, kind=scratch_kind).ap()
    v_scr = nc.dram_tensor("v_scr", [4, 32, P, 512], BF16, kind=scratch_kind).ap()
    a_scr = nc.dram_tensor("a_scr", [8, P, 16, 512], BF16, kind=scratch_kind).ap()
    h1f_scr = nc.dram_tensor("h1f_scr", [NFC, P, S], F32, kind=scratch_kind).ap()
    h1b_scr = nc.dram_tensor("h1b_scr", [NFC, P, S], BF16, kind="Internal").ap()
    hdn_scr = nc.dram_tensor("hdn_scr", [8, P, NI, 512], BF16, kind=scratch_kind).ap()
    dw_bf = nc.dram_tensor("dw_bf", [NFC, P, NI, P], BF16, kind="Internal").ap()
    gw_bf = nc.dram_tensor("gw_bf", [NI, P, NFC * P], BF16, kind="Internal").ap()
    uw_bf = nc.dram_tensor("uw_bf", [NI, P, NFC * P], BF16, kind="Internal").ap()
    dbg = nc.dram_tensor("dbg", [P, 64], F32, kind=scratch_kind).ap() if debug else None
    dbg2 = nc.dram_tensor("dbg2", [P, 2, 512], F32, kind=scratch_kind).ap() if debug else None
    dbg3 = nc.dram_tensor("dbg3", [P, 32, 257], F32, kind=scratch_kind).ap() if debug else None
    dbg4 = nc.dram_tensor("dbg4", [P, 2, 512], F32, kind=scratch_kind).ap() if debug else None
    dbg5 = nc.dram_tensor("dbg5", [P, 4, 257], F32, kind=scratch_kind).ap() if debug else None
    dbg6 = nc.dram_tensor("dbg6", [P, 2, 256], F32, kind=scratch_kind).ap() if debug else None

    ident_h = nc.inline_tensor(np.eye(P, dtype=ml_dtypes.bfloat16), name="identc_i")
    onesc_h = nc.inline_tensor(np.ones((P, 1), dtype=ml_dtypes.bfloat16), name="onescol_i")
    onesr_h = nc.inline_tensor(np.ones((1, P), dtype=np.float32), name="onesrow_i")
    negr_h = nc.inline_tensor(np.full((1, P), -1.0, dtype=np.float32), name="negrow_i")
    m_np = np.zeros((P, 2, 512), dtype=ml_dtypes.bfloat16)
    for d_ in range(2):
        pat = ((np.arange(P)[:, None] + 128 * d_) <= np.arange(256)[None, :])
        m_np[:, d_, 0:256] = pat.astype(ml_dtypes.bfloat16)
        m_np[:, d_, 256:512] = pat.astype(ml_dtypes.bfloat16)
    mask_h = nc.inline_tensor(m_np, name="maskc_i")

    ident = nc.alloc_sbuf_tensor("ident", [P, P], BF16)
    onescol = nc.alloc_sbuf_tensor("onescol", [P, 1], BF16)
    onesrow = nc.alloc_sbuf_tensor("onesrow", [1, P], F32)
    negrow = nc.alloc_sbuf_tensor("negrow", [1, P], F32)
    masks = nc.alloc_sbuf_tensor("masks", [P, 2, 512], BF16)
    lam_sb = nc.alloc_sbuf_tensor("lam_sb", [1, 512], F32)
    w2_sb = nc.alloc_sbuf_tensor("w2_sb", [P, 2], F32)
    gs_sb = nc.alloc_sbuf_tensor("gs_sb", [P, NI], F32)
    us_sb = nc.alloc_sbuf_tensor("us_sb", [P, NI], F32)
    ds_sb = nc.alloc_sbuf_tensor("ds_sb", [P, NFC], F32)
    zeroc = nc.alloc_sbuf_tensor("zeroc", [P, 1], F32)
    epsc = nc.alloc_sbuf_tensor("epsc", [P, 1], F32)
    zero1 = nc.alloc_sbuf_tensor("zero1", [1, 1], F32)
    eps1 = nc.alloc_sbuf_tensor("eps1", [1, 1], F32)
    c0r = nc.alloc_sbuf_tensor("c0r", [1, 1], F32)
    m60 = nc.alloc_sbuf_tensor("m60", [1, 1], F32)
    nmt = nc.alloc_sbuf_tensor("nmt", [1, 256], F32)
    pmax = nc.alloc_sbuf_tensor("pmax", [1, 32], F32)
    prod = nc.alloc_sbuf_tensor("prod", [1, 16], F32)
    lamp = nc.alloc_sbuf_tensor("lamp", [1, 256], F32)
    w2b = nc.alloc_sbuf_tensor("w2b", [P, 2], F32)
    tmp11 = nc.alloc_sbuf_tensor("tmp11", [1, 8], F32)
    expbias = nc.alloc_sbuf_tensor("expbias", [P, 1], F32)
    lam_bc = nc.alloc_sbuf_tensor("lam_bc", [P, 1], F32)
    dumv = nc.alloc_sbuf_tensor("dumv", [1, 1], F32)
    dums = nc.alloc_sbuf_tensor("dums", [1, 1], F32)
    m_row = nc.alloc_sbuf_tensor("m_row", [1, 512], F32)
    vv_row = nc.alloc_sbuf_tensor("vv_row", [1, 512], F32)
    r_row = nc.alloc_sbuf_tensor("r_row", [1, 512], F32)
    b_row = nc.alloc_sbuf_tensor("b_row", [1, 512], F32)

    ar = Arena(nc, _ceil32(nc.sbuf_base), nc.sbuf_top)

    ar.reset()
    xt_bf = ar.alloc("xt_bf", [P, NFC, S], BF16)
    wqk_t = [ar.alloc("wqk_t", [P, NFC, P], BF16) for _ in range(2)]
    wv_t = [ar.alloc("wv_t", [P, NFC, 512], BF16) for _ in range(2)]
    stg = [ar.alloc("stg", [P, 512], BF16) for _ in range(4)]
    sqb = [ar.alloc("sqb", [P, 512], BF16) for _ in range(2)]

    ar.reset()
    qk_buf = [[ar.alloc("qk", [P, S], BF16) for _ in range(4)] for _ in range(2)]
    v_buf = [ar.alloc("vbuf", [P, 32, 257], BF16) for _ in range(2)]
    p_t = [ar.alloc("pt", [P, 512], BF16) for _ in range(NPT)]
    pmask_t = [ar.alloc("pmt", [P, 512], BF16) for _ in range(2)]
    o_c = [[ar.alloc("oc", [P, 257], F32) for _ in range(2)] for _ in range(2)]
    t2_sb = [ar.alloc("t2s", [P, 256], F32) for _ in range(2)]
    a_sb = [ar.alloc("a_sb", [P, 256], F32) for _ in range(2)]
    sqe = [ar.alloc("sqe", [P, 256], F32) for _ in range(2)]
    r1s = [ar.alloc("r1s", [P, 1], F32) for _ in range(2)]
    r2s = [ar.alloc("r2s", [P, 1], F32) for _ in range(2)]
    r2l = [ar.alloc("r2l", [P, 1], F32) for _ in range(2)]
    ssm = [ar.alloc("ssm", [P, 1], F32) for _ in range(2)]
    rst = [ar.alloc("rst", [P, 1], F32) for _ in range(2)]
    an_sb = [[ar.alloc("an", [P, 256], BF16) for _ in range(2)] for _ in range(3)]
    at_stage = [ar.alloc("at_st", [P, 512], BF16) for _ in range(2)]
    dwc_stage = [ar.alloc("dwc", [P, NI * P], BF16) for _ in range(2)]

    ar.reset()
    wo_sb = ar.alloc("wo_sb", [P, NFC, D], BF16)
    at_t = [ar.alloc("at_t", [P, NFC, 512], BF16) for _ in range(2)]
    xt_r = [ar.alloc("xt_r", [P, 512], F32) for _ in range(2)]
    xres = [ar.alloc("xres", [P, NFC, 512], F32) for _ in range(2)]
    xrb = [ar.alloc("xrb", [P, 512], BF16) for _ in range(2)]
    sq3 = [ar.alloc("sq3", [P, 512], BF16) for _ in range(2)]
    ln_tmp = [ar.alloc("ln_tmp", [P, 512], F32) for _ in range(2)]
    wo_stg = [ar.alloc("wo_stg", [P, D], BF16) for _ in range(2)]
    h1f_st = [ar.alloc("h1f_st", [P, 512], F32) for _ in range(2)]
    h1b_st = [ar.alloc("h1b_st", [P, 512], BF16) for _ in range(2)]

    ar.reset()
    h1_pl = ar.alloc("h1_pl", [P, NFC, S], BF16)
    gw_t = [ar.alloc("gw_t", [P, NFC, P], BF16) for _ in range(4)]
    uw_t = [ar.alloc("uw_t", [P, NFC, P], BF16) for _ in range(4)]
    sg_sb = [ar.alloc("sg", [P, 512], F32) for _ in range(2)]
    uu_sb = [ar.alloc("uu", [P, 512], F32) for _ in range(2)]
    hdn_st = [ar.alloc("hdn_st", [P, 512], BF16) for _ in range(4)]

    ar.reset()
    hdn_t = [ar.alloc("hdn_t", [P, NI, 512], BF16) for _ in range(2)]
    dw_t = [ar.alloc("dw_t", [P, NI, P], BF16) for _ in range(2)]
    h1r = [ar.alloc("h1r", [P, 512], F32) for _ in range(2)]
    xres2 = [ar.alloc("xres2", [P, NFC, 512], F32) for _ in range(2)]
    xrb4 = [ar.alloc("xrb4", [P, 512], BF16) for _ in range(2)]
    sq4 = [ar.alloc("sq4", [P, 512], BF16) for _ in range(2)]
    ln_tmp4 = [ar.alloc("ln_tmp4", [P, 512], F32) for _ in range(2)]
    y_st = [ar.alloc("y_st", [P, 512], F32) for _ in range(2)]

    ps_f = [nc.alloc_psum_tensor(f"psf{i}", [P, 512], F32) for i in range(7)]
    ps_t = nc.alloc_psum_tensor("pst", [P, 512], BF16)

    import contextlib
    es = contextlib.ExitStack()
    sems = {n: es.enter_context(nc.semaphore(n)) for n in SEM_NAMES}
    C0 = math.log(1.02 * SCALE)

    def emit_all(E: Em):
        t = E.eng if E.name == "tensor" else None
        v = E.eng if E.name == "vector" else None
        s = E.eng if E.name == "scalar" else None

        def phase_barrier():
            for sem in sorted(E.dma_used):
                E.w(E.name, sem)
            inst = None
            if t:
                inst = t.matmul(ps_f[6][0:1, 0:1], onesrow[0:1, 0:1],
                                onesrow[0:1, 0:1], start=True, stop=True)
            E.i("bar", inst)
            inst = None
            if v:
                inst = v.tensor_copy(out=dumv[:], in_=onesrow[0:1, 0:1])
            E.i("bar", inst)
            inst = None
            if s:
                inst = s.activation(out=dums[:], in_=onesrow[0:1, 0:1], func=AF.Copy,
                                    bias=0.0, scale=1.0)
            E.i("bar", inst)
            E.eng.wait_ge(E.sems["bar"], E.c["bar"])

        E.dma("sync", "stg0", ident[:], ident_h.ap())
        E.dma("sync", "stg0", onescol[:], onesc_h.ap())
        E.dma("sync", "stg0", onesrow[:], onesr_h.ap())
        E.dma("sync", "stg0", negrow[:], negr_h.ap())
        E.dma("sync", "stg0", masks[:], mask_h.ap())
        E.dma("sync", "stg0", lam_sb[:], lam)
        E.dma("sync", "stg0", w2_sb[:], subw2)
        E.dma("sync", "stg0", gs_sb[:], gs_t)
        E.dma("sync", "stg0", us_sb[:], us_t)
        E.dma("sync", "stg0", ds_sb[:], ds_t)
        if v:
            v.memset(zeroc[:], 0.0)
            v.memset(epsc[:], EPS)
            v.memset(zero1[:], 0.0)
            v.memset(eps1[:], EPS)
            v.memset(c0r[:], C0)
            v.memset(m60[:], -60.0)
            inst = v.memset(nmt[:], 0.0)
        else:
            inst = None
        E.i("red", inst)
        E.w("vector", "stg0")
        if v:
            inst = v.tensor_scalar_mul(w2b[:], w2_sb[:], 1.0 - LAMBDA_INIT)
        else:
            inst = None
        E.i("red", inst)
        E.w("scalar", "red")
        E.w("tensor", "stg0")
        E.dma("gpsimd", "w_ld0", wqk_t[0][:], wq[0])
        E.dma("gpsimd", "w_ld1", wqk_t[1][:], wq[1])
        xr_ap = xT.rearrange("(c p) n -> p c n", p=P)
        xt_pool = ["xt_ld", "hl0", "hl1", "at_ld0",
                   "at_ld1", "hdn_ld0", "hdn_ld1", "dwt0"]
        xt_tg = []
        for tc_ in range(8):
            sm = xt_pool[tc_]
            E.dma("gpsimd", sm, xt_bf[:, :, tc_ * 512:(tc_ + 1) * 512],
                  xr_ap[:, :, tc_ * 512:(tc_ + 1) * 512])
            xt_tg.append((sm, E.c[SEM_ALIAS.get(sm, sm)]))

        grp = 0
        for pi, (w_ap, scr) in enumerate([(wq, q_scr), (wk, k_scr)]):
            for oc in range(16):
                li = pi * 16 + oc
                slot = li % 2
                wsem = f"w_ld{slot}"
                if li >= 2:
                    E.w("gpsimd", "grp", slack=8)
                    E.dma("gpsimd", wsem, wqk_t[slot][:], w_ap[oc])
                E.w("tensor", wsem)
                for tc in range(8):
                    if li == 0:
                        E.wt("tensor", xt_tg[tc][0], xt_tg[tc][1])
                    bank = grp % 2
                    E.w("tensor", "sq_done", slack=1)
                    last = None
                    for c in range(NFC):
                        last = E.mm(ps_f[bank][:], wqk_t[slot][:, c, :],
                                    xt_bf[:, c, tc * 512:(tc + 1) * 512],
                                    start=(c == 0), stop=(c == NFC - 1))
                    E.i("grp", last)
                    if grp > 0:
                        E.w("tensor", "sq_done")
                        E.w("tensor", "red")
                        E.mm(ps_f[2][0:1, :], onescol[:], sqb[(grp - 1) % 2][:],
                             start=True, stop=True, sem="statmm")
                    E.w("scalar", "grp")
                    stslot = grp % 4
                    E.w("scalar", f"stg{stslot}")
                    cinst = None
                    if s:
                        cinst = s.activation(out=stg[stslot][:], in_=ps_f[bank][:],
                                             func=AF.Copy, bias=0.0, scale=1.0)
                    E.i(f"stgr{stslot}", cinst)
                    E.w("scalar", "statmm")
                    inst = None
                    if s:
                        inst = s.activation(out=sqb[grp % 2][:], in_=ps_f[bank][:],
                                            func=AF.Square, bias=zeroc[:], scale=1.0)
                    E.i("sq_done", inst)
                    if grp > 0:
                        E.w("vector", "statmm")
                        inst = None
                        if v:
                            inst = v.reduce_max(nmt[0:1, grp - 1:grp], ps_f[2][0:1, :],
                                                axis=AX.X)
                        E.i("red", inst)
                    E.w("sync", f"stgr{stslot}")
                    E.dma("sync", f"stg{stslot}", scr[oc][:, tc * 512:(tc + 1) * 512],
                          stg[stslot][:])
                    grp += 1
        E.w("tensor", "sq_done")
        E.w("tensor", "red")
        E.mm(ps_f[2][0:1, :], onescol[:], sqb[(grp - 1) % 2][:], start=True, stop=True,
             sem="statmm")
        E.w("vector", "statmm")
        inst = None
        if v:
            inst = v.reduce_max(nmt[0:1, grp - 1:grp], ps_f[2][0:1, :], axis=AX.X)
        E.i("red", inst)

        if v:
            for pl in range(32):
                v.reduce_max(pmax[0:1, pl:pl + 1], nmt[0:1, pl * 8:(pl + 1) * 8],
                             axis=AX.X)
            for pl in range(16):
                v.tensor_mul(prod[0:1, pl:pl + 1], pmax[0:1, pl:pl + 1],
                             pmax[0:1, 16 + pl:17 + pl])
            v.tensor_mul(lamp[0:1, 0:P], lam_sb[0:1, 0:P], lam_sb[0:1, P:2 * P])
            v.tensor_mul(lamp[0:1, P:2 * P], lam_sb[0:1, 2 * P:3 * P],
                         lam_sb[0:1, 3 * P:4 * P])
            v.reduce_max(tmp11[0:1, 1:2], prod[0:1, 0:16], axis=AX.X)
            v.reduce_sum(tmp11[0:1, 2:3], lamp[0:1, 0:P], axis=AX.X)
            inst = v.reduce_sum(tmp11[0:1, 3:4], lamp[0:1, P:2 * P], axis=AX.X)
        else:
            inst = None
        E.i("vmax", inst)
        E.w("scalar", "vmax")
        inst = None
        if s:
            s.activation(out=tmp11[0:1, 1:2], in_=tmp11[0:1, 1:2], func=AF.Ln,
                         bias=zero1[:], scale=1.0)
            s.activation(out=tmp11[0:1, 1:2], in_=tmp11[0:1, 1:2], func=AF.Exp,
                         bias=c0r[:], scale=0.5)
            s.activation(out=tmp11[0:1, 1:2], in_=tmp11[0:1, 1:2], func=AF.Relu,
                         bias=m60[:], scale=1.0)
            s.activation(out=tmp11[0:1, 2:3], in_=tmp11[0:1, 2:3], func=AF.Exp,
                         bias=zero1[:], scale=1.0)
            inst = s.activation(out=tmp11[0:1, 3:4], in_=tmp11[0:1, 3:4], func=AF.Exp,
                                bias=zero1[:], scale=1.0)
        E.i("mhat", inst)
        E.w("vector", "mhat")
        inst = None
        if v:
            v.tensor_sub(tmp11[0:1, 6:7], tmp11[0:1, 2:3], tmp11[0:1, 3:4])
            v.tensor_scalar_mul(tmp11[0:1, 1:2], tmp11[0:1, 1:2], -1.0)
            inst = v.tensor_scalar_add(tmp11[0:1, 6:7], tmp11[0:1, 6:7], LAMBDA_INIT)
        E.i("lamf", inst)
        E.w("tensor", "lamf")
        E.mm(ps_f[6][:, 0:1], onesrow[:], tmp11[0:1, 1:2], start=True, stop=True)
        E.mm(ps_f[6][:, 1:2], onesrow[:], tmp11[0:1, 6:7], start=True, stop=True, sem="bc")
        E.w("vector", "bc")
        inst = None
        if v:
            v.tensor_copy(out=expbias[:], in_=ps_f[6][:, 0:1])
            inst = v.tensor_copy(out=lam_bc[:], in_=ps_f[6][:, 1:2])
        E.i("prep", inst)
        if debug:
            E.w("sync", "prep")
            E.dma("sync", "stg0", dbg[:, 0:1], expbias[:], allow_slow_non_contiguous=True)
            E.dma("sync", "stg0", dbg[:, 1:2], lam_bc[:], allow_slow_non_contiguous=True)
            E.dma("gpsimd", "xt_ld", dbg2[:], masks[:])

        E.w("tensor", "sq_done")
        vg = 0
        for hp in range(4):
            slot = hp % 2
            wsem = f"wv_ld{slot}"
            E.w("gpsimd", "grp", slack=32)
            E.dma("gpsimd", wsem, wv_t[slot][:], wv[hp])
            E.w("tensor", wsem)
            for tc in range(32):
                bank = vg % 2
                E.w("tensor", "stgr", slack=1)
                last = None
                for c in range(NFC):
                    last = E.mm(ps_f[bank][:], xt_bf[:, c, tc * P:(tc + 1) * P],
                                wv_t[slot][:, c, :], start=(c == 0), stop=(c == NFC - 1))
                E.i("grp", last)
                E.w("scalar", "grp")
                stslot = vg % 4
                E.w("scalar", f"stg{stslot}")
                inst = None
                if s:
                    inst = s.activation(out=stg[stslot][:], in_=ps_f[bank][:], func=AF.Copy,
                                        bias=0.0, scale=1.0)
                E.i(f"stgr{stslot}", inst)
                E.w("sync", f"stgr{stslot}")
                E.dma("sync", f"stg{stslot}", v_scr[hp, tc][:], stg[stslot][:])
                vg += 1

        phase_barrier()

        inst = None
        if v:
            v.memset(v_buf[0][:, :, 256:257], 1.0)
            inst = v.memset(v_buf[1][:, :, 256:257], 1.0)
        E.i("prep", inst)
        E.w("scalar", "prep")
        E.w("tensor", "prep")

        def head_load_items(h):
            slot = h % 2
            sem = f"hl{slot}"
            items = []
            for comp in range(2):
                items.append((sem, qk_buf[slot][comp][:], q_scr[2 * h + comp]))
                items.append((sem, qk_buf[slot][2 + comp][:], k_scr[2 * h + comp]))
            off = (h % 2) * 256
            for tcv in range(32):
                items.append((sem, v_buf[slot][:, tcv, 0:256],
                              v_scr[h // 2, tcv][:, off:off + 256]))
            return items

        def head_load(h):
            E.w("sync", "obfree")
            for sem, out, in_ in head_load_items(h):
                E.dma("sync", sem, out, in_)

        pend_tail = []
        pend_act = []

        def emit_act2(st):
            hh, jj, jg_, ssum_tgt = st
            E.wt("scalar", "ssum", ssum_tgt)
            inst = None
            if s:
                s.activation(out=rst[0][:], in_=ssm[0][:], func=AF.Ln, bias=epsc[:],
                             scale=1.0 / 256.0)
                s.activation(out=rst[1][:], in_=ssm[1][:], func=AF.Ln, bias=epsc[:],
                             scale=1.0 / 256.0)
                s.activation(out=rst[0][:], in_=rst[0][:], func=AF.Exp,
                             bias=zeroc[:], scale=-0.5)
                inst = s.activation(out=rst[1][:], in_=rst[1][:], func=AF.Exp,
                                    bias=zeroc[:], scale=-0.5)
            E.i("rstd", inst)
            E.w("vector", "rstd")
            E.w("vector", "tp")
            inst = None
            if v:
                v.tensor_scalar(out=an_sb[jg_ % 3][0][:], in0=a_sb[0][:],
                                scalar1=rst[0][:], scalar2=None, op0=ALU.mult)
                inst = v.tensor_scalar(out=an_sb[jg_ % 3][1][:], in0=a_sb[1][:],
                                       scalar1=rst[1][:], scalar2=None, op0=ALU.mult)
            E.i("an", inst)
            pend_tail.append((hh, jj, jg_, E.c["an"]))

        pc_jobs = ([("dw", ho_) for ho_ in range(NFC)]
                   + [("gw", i_) for i_ in range(NI)]
                   + [("uw", i_) for i_ in range(NI)])
        pc_sched = {h_: pc_jobs[(len(pc_jobs) * h_) // H:(len(pc_jobs) * (h_ + 1)) // H]
                    for h_ in range(H)}

        def emit_tail2(st):
            hh, jj, jg_, an_tgt = st
            E.wt("tensor", "an", an_tgt)
            E.w("tensor", "atcp")
            for sl in range(2):
                last = None
                if t:
                    t.transpose(ps_t[:, sl * P:(sl + 1) * P],
                                an_sb[jg_ % 3][sl][:, 0:P], ident[:])
                    last = t.transpose(ps_t[:, 256 + sl * P:256 + (sl + 1) * P],
                                       an_sb[jg_ % 3][sl][:, P:2 * P], ident[:])
                E.i("tp", last)
            E.w("vector", "tp")
            E.w("vector", "ast")
            inst = None
            if v:
                inst = v.tensor_copy(out=at_stage[jj % 2][:], in_=ps_t[:])
            E.i("atcp", inst)
            E.w("sync", "atcp")
            tj, oj = (jj * 256) // 512, (jj * 256) % 512
            E.dma("sync", "ast", a_scr[tj][:, 2 * hh, oj:oj + 256],
                  at_stage[jj % 2][:, 0:256])
            E.dma("sync", "ast", a_scr[tj][:, 2 * hh + 1, oj:oj + 256],
                  at_stage[jj % 2][:, 256:512])

        ktg = [0]
        head_load(0)
        if debug:
            E.w("gpsimd", "hl0")
            E.dma("gpsimd", "xt_ld", dbg3[:], v_buf[0][:])

        def emit_pv(st):
            c_, srct_, prt_, pmt_, nkt_, vc_ = st
            if pmt_ is not None:
                E.wt("tensor", "pm", pmt_)
            else:
                E.wt("tensor", "pr", prt_)
            if c_ == 0:
                E.w("tensor", "obfree")
            last = None
            for b_ in range(2):
                for sl in range(2):
                    last = E.mm(ps_f[2 + 2 * b_ + sl][:, 0:257],
                                srct_[:, b_ * 256 + sl * P:b_ * 256 + (sl + 1) * P],
                                vc_[:, c_, :], start=(c_ == 0),
                                stop=(c_ == nkt_ - 1))
            E.i(f"psf{c_ % NPT}", last)

        def emit_epilogue(st):
            hh, jj = st
            nkt_ = 2 * jj + 2
            lastslot = (nkt_ - 1) % NPT
            E.w("vector", f"psf{lastslot}")
            insts = [None, None, None, None]
            if v:
                insts[0] = v.tensor_copy(out=o_c[0][0][:], in_=ps_f[2][:, 0:257])
                insts[1] = v.tensor_copy(out=o_c[0][1][:], in_=ps_f[3][:, 0:257])
                insts[2] = v.tensor_copy(out=o_c[1][0][:], in_=ps_f[4][:, 0:257])
                insts[3] = v.tensor_copy(out=o_c[1][1][:], in_=ps_f[5][:, 0:257])
            for q_ in range(4):
                E.i("obfree", insts[q_])
            if debug and hh == 0 and jj == 0:
                E.w("sync", "obfree")
                for b_ in range(2):
                    for sl_ in range(2):
                        E.dma("sync", "stg0", dbg5[:, 2 * b_ + sl_, :], o_c[b_][sl_][:])
            inst = None
            if v:
                v.reciprocal(r2s[0][:], o_c[1][0][:, 256:257])
                v.reciprocal(r2s[1][:], o_c[1][1][:, 256:257])
                v.reciprocal(r1s[0][:], o_c[0][0][:, 256:257])
                v.reciprocal(r1s[1][:], o_c[0][1][:, 256:257])
                v.tensor_mul(r2l[0][:], r2s[0][:], lam_bc[:])
                v.tensor_mul(r2l[1][:], r2s[1][:], lam_bc[:])
                v.tensor_scalar(out=t2_sb[0][:], in0=o_c[1][0][:, 0:256],
                                scalar1=r2l[0][:], scalar2=None, op0=ALU.mult)
                v.tensor_scalar(out=t2_sb[1][:], in0=o_c[1][1][:, 0:256],
                                scalar1=r2l[1][:], scalar2=None, op0=ALU.mult)
                v.scalar_tensor_tensor(out=a_sb[0][:], in0=o_c[0][0][:, 0:256],
                                       scalar=r1s[0][:], in1=t2_sb[0][:],
                                       op0=ALU.mult, op1=ALU.subtract)
                inst = v.scalar_tensor_tensor(out=a_sb[1][:], in0=o_c[0][1][:, 0:256],
                                              scalar=r1s[1][:], in1=t2_sb[1][:],
                                              op0=ALU.mult, op1=ALU.subtract)
            E.i("oeread", inst)
            inst = None
            if v:
                v.tensor_mul(sqe[0][:], a_sb[0][:], a_sb[0][:])
                v.tensor_mul(sqe[1][:], a_sb[1][:], a_sb[1][:])
                v.reduce_sum(ssm[0][:], sqe[0][:], axis=AX.X)
                inst = v.reduce_sum(ssm[1][:], sqe[1][:], axis=AX.X)
            E.i("ssum", inst)
            pend_act.append((hh, jj, 16 * hh + jj, E.c["ssum"]))

        pend_epi = []
        hl_pend = []
        carries = []
        for h in range(H):
            slot = h % 2
            hsem = f"hl{slot}"
            q1, q2, k1, k2 = qk_buf[slot]
            vc = v_buf[slot]
            E.w("tensor", hsem)
            E.w("scalar", hsem)
            assert not hl_pend
            if h + 1 < H:
                hl_pend = head_load_items(h + 1)
            for jn, (kind, idx) in enumerate(pc_sched[h]):
                dslot = (h * 23 + jn) % 2
                E.w("gpsimd", f"dwc_st{dslot}")
                if kind == "dw":
                    E.dma("gpsimd", f"dwc_ld{dslot}", dwc_stage[dslot][:], dwT[idx])
                    E.w("gpsimd", f"dwc_ld{dslot}")
                    E.dma("gpsimd", f"dwc_st{dslot}", dw_bf[idx], dwc_stage[dslot][:])
                elif kind == "gw":
                    E.dma("gpsimd", f"dwc_ld{dslot}", dwc_stage[dslot][:, 0:D], gwT[idx])
                    E.w("gpsimd", f"dwc_ld{dslot}")
                    E.dma("gpsimd", f"dwc_st{dslot}", gw_bf[idx], dwc_stage[dslot][:, 0:D])
                else:
                    E.dma("gpsimd", f"dwc_ld{dslot}", dwc_stage[dslot][:, 0:D], uwT[idx])
                    E.w("gpsimd", f"dwc_ld{dslot}")
                    E.dma("gpsimd", f"dwc_st{dslot}", uw_bf[idx], dwc_stage[dslot][:, 0:D])

            for j in range(NTQ):
                nkt = 2 * j + 2
                for c in range(nkt):
                    bank = (0, 1, 6)[ktg[0] % 3]
                    ktg[0] += 1
                    scg = "scg0"
                    pslot = c % NPT
                    diag = c >= 2 * j
                    if c < 2 and len(carries) >= 2:
                        emit_pv(carries.pop(0))
                    E.w("tensor", "pr", slack=2)
                    E.mm(ps_f[bank][:, 0:256], k1[:, c * P:(c + 1) * P],
                         q1[:, j * 256:(j + 1) * 256], start=True, stop=True)
                    E.mm(ps_f[bank][:, 256:512], k2[:, c * P:(c + 1) * P],
                         q2[:, j * 256:(j + 1) * 256], start=True, stop=True, sem=scg)
                    E.w("scalar", scg)
                    E.w("scalar", f"psf{pslot}")
                    inst = None
                    if s:
                        inst = s.activation(out=p_t[pslot][:], in_=ps_f[bank][:],
                                            func=AF.Exp, bias=expbias[:], scale=SCALE)
                    E.i(f"pr{pslot}", inst)
                    pr_tgt = E.c["pr"]
                    pm_tgt = None
                    if diag:
                        dsl = c - 2 * j
                        E.w("vector", f"pr{pslot}")
                        E.w("vector", "obfree")
                        inst = None
                        if v:
                            inst = v.tensor_mul(pmask_t[dsl][:], p_t[pslot][:],
                                                masks[:, dsl, :])
                        E.i(f"pm{pslot}", inst)
                        pm_tgt = E.c["pm"]
                        src_t = pmask_t[dsl]
                    else:
                        src_t = p_t[pslot]
                    if c >= 2 and len(carries) >= 2:
                        emit_pv(carries.pop(0))
                    carries.append((c, src_t, pr_tgt, pm_tgt, nkt, vc))
                    if c == 1 and pend_epi:
                        emit_epilogue(pend_epi.pop(0))
                    if c == min(3, nkt - 1) and pend_act:
                        emit_act2(pend_act.pop(0))
                    if c == min(4, nkt - 1) and len(pend_tail) >= 2:
                        emit_tail2(pend_tail.pop(0))
                pend_epi.append((h, j))
                for _ in range(3):
                    if hl_pend:
                        sem_, out_, in_ = hl_pend.pop(0)
                        E.w("sync", "obfree")
                        E.dma("sync", sem_, out_, in_)

        while carries:
            emit_pv(carries.pop(0))
        while pend_epi:
            emit_epilogue(pend_epi.pop(0))
        while pend_act:
            emit_act2(pend_act.pop(0))
        while pend_tail:
            emit_tail2(pend_tail.pop(0))

        phase_barrier()

        E.dma("gpsimd", "wo_ld", wo_sb[:], wo.rearrange("(c p) n -> p c n", p=P))
        E.w("vector", "wo_ld")
        inst = None
        if v:
            for c in range(NFC + 1):
                if c < NFC:
                    v.tensor_scalar(out=wo_stg[c % 2][:], in0=wo_sb[:, c, :],
                                    scalar1=w2b[:, c % 2:c % 2 + 1], scalar2=None,
                                    op0=ALU.mult)
                if c >= 1:
                    inst = v.tensor_copy(out=wo_sb[:, c - 1, :], in_=wo_stg[(c - 1) % 2][:])
        E.i("wo_sc", inst)
        E.w("tensor", "wo_sc")

        def ln_rows(rows, s_stat, s_rows, s_bc, s_app):
            m_r, vv_r, r_r, b_r = rows
            E.w("tensor", s_app)
            E.w("vector", s_stat)
            inst = None
            if v:
                v.tensor_scalar_mul(m_r[:], ps_f[2][0:1, :], 1.0 / D)
                v.tensor_mul(b_r[:], m_r[:], m_r[:])
                v.tensor_copy(out=dumv[:], in_=onesrow[0:1, 0:1])
                inst = v.scalar_tensor_tensor(out=vv_r[:], in0=ps_f[3][0:1, :],
                                              scalar=1.0 / D, in1=b_r[:],
                                              op0=ALU.mult, op1=ALU.subtract)
            E.i(s_rows, inst)
            E.w("scalar", s_rows)
            inst = None
            if s:
                s.activation(out=r_r[:], in_=vv_r[:], func=AF.Ln, bias=eps1[:], scale=1.0)
                inst = s.activation(out=r_r[:], in_=r_r[:], func=AF.Exp, bias=zero1[:],
                                    scale=-0.5)
            E.i(s_rows, inst)
            E.w("vector", s_rows)
            inst = None
            if v:
                inst = v.tensor_mul(b_r[:], m_r[:], r_r[:])
            E.i(s_rows, inst)
            E.w("tensor", s_rows)
            E.mm(ps_f[4][:], onesrow[:], r_r[:], start=True, stop=True)
            E.mm(ps_f[5][:], negrow[:], b_r[:], start=True, stop=True, sem=s_bc)
            E.w("vector", s_bc)

        for tcg in range(9):
            aslot = tcg % 2
            if tcg == 0:
                E.dma("sync", "at_ld0", at_t[0][:], a_scr[0])
            if tcg < 8:
                E.w("tensor", f"at_ld{aslot}")
            if tcg + 1 < 8:
                nslot = (tcg + 1) % 2
                E.w("sync", "mmf")
                E.dma("sync", f"at_ld{nslot}", at_t[nslot][:], a_scr[tcg + 1])
            if tcg >= 1:
                ln_rows((m_row, vv_row, r_row, b_row), "stat2", "rows", "bc2", "app")
            for fo in range(NFC + 2):
                if tcg < 8:
                    if fo < NFC:
                        bank = fo % 2
                        xslot = fo % 2
                        E.w("gpsimd", f"h1g{xslot}")
                        E.dma("gpsimd", f"xt_r{xslot}", xt_r[xslot][:],
                              xT[fo * P:(fo + 1) * P, tcg * 512:(tcg + 1) * 512])
                        E.w("tensor", "evf")
                        if fo == 0:
                            E.w("tensor", "rows")
                        last = None
                        for c in range(NFC):
                            last = E.mm(ps_f[bank][:], wo_sb[:, c, fo * P:(fo + 1) * P],
                                        at_t[aslot][:, c, :], start=(c == 0),
                                        stop=(c == NFC - 1))
                        E.i("mmf", last)
                    if fo >= 2:
                        f2 = fo - 2
                        E.w("tensor", "evf")
                        E.mm(ps_f[2][0:1, :], onescol[:], xrb[f2 % 2][:],
                             start=(f2 == 0), stop=(f2 == NFC - 1))
                        E.mm(ps_f[3][0:1, :], onescol[:], sq3[f2 % 2][:],
                             start=(f2 == 0), stop=(f2 == NFC - 1),
                             sem=("stat2" if f2 == NFC - 1 else None))
                    if 1 <= fo <= NFC:
                        f1 = fo - 1
                        minst = None
                        if v:
                            v.tensor_copy(out=xrb[f1 % 2][:], in_=xres[tcg % 2][:, f1, :])
                            minst = v.tensor_mul(sq3[f1 % 2][:], xres[tcg % 2][:, f1, :],
                                                 xres[tcg % 2][:, f1, :])
                        E.i("evf", minst)
                    if fo < NFC:
                        E.w("vector", "mmf")
                        E.w("vector", f"xt_r{xslot}")
                        ainst = None
                        if v:
                            ainst = v.tensor_add(xres[tcg % 2][:, fo, :], ps_f[bank][:],
                                                 xt_r[xslot][:])
                        E.i(f"h1g{xslot}", ainst)
                if tcg >= 1:
                    pc_ = tcg - 1
                    if fo < NFC:
                        if v:
                            v.tensor_mul(ln_tmp[fo % 2][:], xres[pc_ % 2][:, fo, :],
                                         ps_f[4][:])
                    if 1 <= fo <= NFC:
                        f1 = fo - 1
                        E.w("vector", f"y{f1 % 2}")
                        if v:
                            v.tensor_add(h1f_st[f1 % 2][:], ln_tmp[f1 % 2][:], ps_f[5][:])
                    if 2 <= fo:
                        f2 = fo - 2
                        hslot = f2 % 2
                        inst = None
                        if v:
                            inst = v.tensor_copy(out=h1b_st[hslot][:], in_=h1f_st[hslot][:])
                        E.i("app", inst)
                        E.w("gpsimd", "app")
                        E.dma("gpsimd", f"y{hslot}",
                              h1f_scr[f2][:, pc_ * 512:(pc_ + 1) * 512], h1f_st[hslot][:])
                        E.dma("gpsimd", f"y{hslot}",
                              h1b_scr[f2][:, pc_ * 512:(pc_ + 1) * 512], h1b_st[hslot][:])

        phase_barrier()

        for i_ in range(2):
            E.dma("sync", f"gw{i_}", gw_t[i_][:], gw_bf[i_])
            E.dma("sync", f"uw{i_}", uw_t[i_][:], uw_bf[i_])
        h1_pool = ["h1pl", "h1pl2", "hdn_ld0", "hdn_ld1",
                   "dwt0", "dwt1", "at_ld0", "at_ld1"]
        h1_tg = []
        h1r_ap = h1b_scr.rearrange("c p n -> p c n")
        for tc_ in range(8):
            sm = h1_pool[tc_]
            E.dma(("sync", "gpsimd")[tc_ % 2], sm,
                  h1_pl[:, :, tc_ * 512:(tc_ + 1) * 512],
                  h1r_ap[:, :, tc_ * 512:(tc_ + 1) * 512])
            h1_tg.append((sm, E.c[SEM_ALIAS.get(sm, sm)]))
        for i in range(NI):
            gslot = i % 4
            if i >= 2:
                E.w("sync", "gu_grp", slack=24)
                E.dma("sync", f"gw{gslot}", gw_t[gslot][:], gw_bf[i])
                E.dma("sync", f"uw{gslot}", uw_t[gslot][:], uw_bf[i])
            E.w("tensor", f"gw{gslot}")
            E.w("tensor", f"uw{gslot}")
            for tc in range(8):
                if i == 0:
                    E.wt("tensor", h1_tg[tc][0], h1_tg[tc][1])
                gi = i * 8 + tc
                gbank = gi % 2
                ubank = 2 + gi % 2
                E.w("tensor", "sg_done", slack=1)
                E.w("tensor", "uu_done", slack=1)
                for c in range(NFC):
                    E.mm(ps_f[gbank][:], gw_t[gslot][:, c, :],
                         h1_pl[:, c, tc * 512:(tc + 1) * 512],
                         start=(c == 0), stop=(c == NFC - 1))
                lastu = None
                for c in range(NFC):
                    lastu = E.mm(ps_f[ubank][:], uw_t[gslot][:, c, :],
                                 h1_pl[:, c, tc * 512:(tc + 1) * 512],
                                 start=(c == 0), stop=(c == NFC - 1))
                E.i("gu_grp", lastu)
                E.w("scalar", "gu_grp")
                E.w("scalar", "hdn_done")
                inst = None
                if s:
                    inst = s.activation(out=sg_sb[gi % 2][:], in_=ps_f[gbank][:],
                                        func=AF.Silu, bias=zeroc[:],
                                        scale=gs_sb[:, i:i + 1])
                E.i("sg_done", inst)
                E.w("vector", "gu_grp")
                inst = None
                if v:
                    inst = v.tensor_scalar(out=uu_sb[gi % 2][:], in0=ps_f[ubank][:],
                                           scalar1=us_sb[:, i:i + 1], scalar2=None,
                                           op0=ALU.mult)
                E.i("uu_done", inst)
                if gi >= 1:
                    pg = gi - 1
                    pi_, ptc = pg // 8, pg % 8
                    E.w("vector", "sg_done", slack=1)
                    E.w("vector", f"hdng{pg % 4}")
                    inst = None
                    if v:
                        inst = v.tensor_mul(hdn_st[pg % 4][:], sg_sb[pg % 2][:],
                                            uu_sb[pg % 2][:])
                    E.i("hdn_done", inst)
                    E.w("gpsimd", "hdn_done")
                    E.dma("gpsimd", f"hdng{pg % 4}", hdn_scr[ptc][:, pi_, :],
                          hdn_st[pg % 4][:])
        pg = NI * 8 - 1
        pi_, ptc = pg // 8, pg % 8
        E.w("vector", "sg_done")
        E.w("vector", f"hdng{pg % 4}")
        inst = None
        if v:
            inst = v.tensor_mul(hdn_st[pg % 4][:], sg_sb[pg % 2][:], uu_sb[pg % 2][:])
        E.i("hdn_done", inst)
        E.w("gpsimd", "hdn_done")
        E.dma("gpsimd", f"hdng{pg % 4}", hdn_scr[ptc][:, pi_, :], hdn_st[pg % 4][:])

        phase_barrier()

        for tcg in range(9):
            hdslot = tcg % 2
            if tcg == 0:
                E.dma("sync", "hdn_ld0", hdn_t[0][:], hdn_scr[0])
            if tcg < 8:
                E.w("tensor", f"hdn_ld{hdslot}")
            if tcg + 1 < 8:
                nslot = (tcg + 1) % 2
                E.w("sync", "mm4")
                E.dma("sync", f"hdn_ld{nslot}", hdn_t[nslot][:], hdn_scr[tcg + 1])
            if tcg >= 1:
                ln_rows((m_row, vv_row, r_row, b_row), "stat4", "rows4", "bc4", "app4")
            for ho in range(NFC + 2):
                if tcg < 8:
                    if ho < NFC:
                        bank = ho % 2
                        dslot = ho % 2
                        hrslot = ho % 2
                        E.w("sync", "mm4", slack=1)
                        E.dma("sync", f"dwt{dslot}", dw_t[dslot][:], dw_bf[ho])
                        E.w("gpsimd", "ev4")
                        E.dma("gpsimd", f"h1r{hrslot}", h1r[hrslot][:],
                              h1f_scr[ho][:, tcg * 512:(tcg + 1) * 512])
                        E.w("tensor", f"dwt{dslot}")
                        E.w("tensor", "ev4")
                        if ho == 0:
                            E.w("tensor", "rows4")
                        last = None
                        for c in range(NI):
                            last = E.mm(ps_f[bank][:], dw_t[dslot][:, c, :],
                                        hdn_t[hdslot][:, c, :], start=(c == 0),
                                        stop=(c == NI - 1))
                        E.i("mm4", last)
                    if ho >= 2:
                        h2 = ho - 2
                        E.w("tensor", "ev4")
                        E.mm(ps_f[2][0:1, :], onescol[:], xrb4[h2 % 2][:],
                             start=(h2 == 0), stop=(h2 == NFC - 1))
                        E.mm(ps_f[3][0:1, :], onescol[:], sq4[h2 % 2][:],
                             start=(h2 == 0), stop=(h2 == NFC - 1),
                             sem=("stat4" if h2 == NFC - 1 else None))
                    if 1 <= ho <= NFC:
                        h1_ = ho - 1
                        minst = None
                        if v:
                            v.tensor_copy(out=xrb4[h1_ % 2][:],
                                          in_=xres2[tcg % 2][:, h1_, :])
                            minst = v.tensor_mul(sq4[h1_ % 2][:],
                                                 xres2[tcg % 2][:, h1_, :],
                                                 xres2[tcg % 2][:, h1_, :])
                        E.i("ev4", minst)
                    if ho < NFC:
                        E.w("vector", "mm4")
                        E.w("vector", f"h1r{hrslot}")
                        if v:
                            v.scalar_tensor_tensor(out=xres2[tcg % 2][:, ho, :],
                                                   in0=ps_f[bank][:],
                                                   scalar=ds_sb[:, ho:ho + 1],
                                                   in1=h1r[hrslot][:],
                                                   op0=ALU.mult, op1=ALU.add)
                if tcg >= 1 and ho <= NFC:
                    pc_ = tcg - 1
                    if ho < NFC:
                        if v:
                            v.tensor_mul(ln_tmp4[ho % 2][:], xres2[pc_ % 2][:, ho, :],
                                         ps_f[4][:])
                    if ho >= 1:
                        po = ho - 1
                        yslot = po % 2
                        E.w("vector", f"y{yslot}")
                        inst = None
                        if v:
                            inst = v.tensor_add(y_st[yslot][:], ln_tmp4[po % 2][:],
                                                ps_f[5][:])
                        E.i("app4", inst)
                        E.w("gpsimd", "app4")
                        E.dma("gpsimd", f"y{yslot}",
                              yT[po * P:(po + 1) * P, pc_ * 512:(pc_ + 1) * 512],
                              y_st[yslot][:])

        for sem in ("ast", "stg0", "stg1", "stg2", "stg3",
                    "h1pl", "hl0", "hl1", "at_ld0", "at_ld1", "hdn_ld0", "hdn_ld1",
                    "dwt0", "dwt1", "gw0", "gw1", "gw2s", "gw3s",
                    "uw0", "uw1", "uw2s", "uw3s"):
            E.w("sync", sem)
        for sem in ("y0", "y1", "y2", "y3", "dwc_st0", "dwc_st1",
                    "hdng0", "hdng1", "hdng2", "hdng3"):
            E.w("gpsimd", sem)
        for sem in ("xt_ld", "w_ld0", "w_ld1", "wv_ld0", "wv_ld1", "dwc_ld0", "dwc_ld1",
                    "wo_ld", "xt_r0", "xt_r1", "gw0", "gw1", "uw0", "uw1",
                    "h1r0", "h1r1"):
            E.w("gpsimd", sem)
        phase_barrier()

    with nc.Block() as block:
        base_counts = {n: 0 for n in SEM_NAMES}
        passes = {}
        for en in ENGINES:
            cc = dict(base_counts)

            def body(eng, cc=cc, en=en):
                emit_all(Em(nc, en, eng, sems, cc))
            getattr(block, en)(body)
            passes[en] = cc
        ref = passes["sync"]
        for n2, cc in passes.items():
            assert cc == ref, f"pass {n2} counts diverge: " + str(
                {k: (cc[k], ref[k]) for k in cc if cc[k] != ref[k]})

    es.close()
    return nc


_CACHE = {}


def _get_nc(debug=False):
    if debug not in _CACHE:
        _CACHE[debug] = build_nc(debug)
    return _CACHE[debug]


def make_in_maps(inputs):
    x = np.asarray(inputs["L0_attn_x"], dtype=np.float32)
    def tile_w(w, m, nchunks=NFC):
        w = np.asarray(w, dtype=np.float32)
        no = w.shape[1] // m
        return np.ascontiguousarray(w.reshape(nchunks, P, no, m).transpose(2, 1, 0, 3))

    wq = tile_w(inputs["L0_attn_Wq"], P)
    wk = tile_w(inputs["L0_attn_Wk"], P)
    wv = tile_w(inputs["L0_attn_Wv"], 512)
    wo = np.ascontiguousarray(np.asarray(inputs["L0_attn_Wo"], dtype=np.float32))
    lamv = np.concatenate([
        np.asarray(inputs["L0_attn_lambda_q1"], dtype=np.float32),
        np.asarray(inputs["L0_attn_lambda_k1"], dtype=np.float32),
        np.asarray(inputs["L0_attn_lambda_q2"], dtype=np.float32),
        np.asarray(inputs["L0_attn_lambda_k2"], dtype=np.float32),
    ]).reshape(1, 512)
    subw2 = np.ascontiguousarray(
        np.asarray(inputs["L0_attn_subln_weight"], dtype=np.float32).reshape(2, P).T)
    gwT = tile_w(np.asarray(inputs["L0_ffn_gate_w"], dtype=np.float32).T, P)
    uwT = tile_w(np.asarray(inputs["L0_ffn_up_w"], dtype=np.float32).T, P)
    dwT = tile_w(np.asarray(inputs["L0_ffn_down_w"], dtype=np.float32).T, P, nchunks=NI)
    gs_t = np.ascontiguousarray(
        np.asarray(inputs["L0_ffn_gate_s"], dtype=np.float32).reshape(NI, P).T)
    us_t = np.ascontiguousarray(
        np.asarray(inputs["L0_ffn_up_s"], dtype=np.float32).reshape(NI, P).T)
    ds_t = np.ascontiguousarray(
        np.asarray(inputs["L0_ffn_down_s"], dtype=np.float32).reshape(NFC, P).T)
    in_maps = []
    for c in range(x.shape[0]):
        xTc = np.ascontiguousarray(x[c].T)
        in_maps.append({
            "xT": xTc, "wq": wq, "wk": wk, "wv": wv, "wo": wo,
            "gwT": gwT, "uwT": uwT, "dwT": dwT,
            "lam": lamv, "subw2": subw2, "gs_t": gs_t, "us_t": us_t, "ds_t": ds_t,
        })
    return in_maps


def kernel(**inputs):
    nc = _get_nc(False)
    in_maps = make_in_maps(inputs)
    res = run_bass_kernel_spmd(nc, in_maps, core_ids=list(range(8)))
    out = np.stack([np.asarray(res.results[c]["yT"]).T for c in range(8)])
    return np.ascontiguousarray(out)
